# version 07
# baseline (speedup 1.0000x reference)
import math
from contextlib import ExitStack

import numpy as np
import concourse.bass as bass
import concourse.mybir as mybir
from concourse.bass_utils import run_bass_kernel_spmd
from concourse.ap import AP

F32 = mybir.dt.float32
BF16 = mybir.dt.bfloat16
I32 = mybir.dt.int32
AF = mybir.ActivationFunctionType
ALU = mybir.AluOpType
AX = mybir.AxisListType

D = 1024
FC = D // 128


class Cfg:
    def __init__(self, T=4096, E=8, HID=4096, S=2048, NH=16, emit=(1, 2, 3), taps=()):
        self.T = T
        self.E = E
        self.HID = HID
        self.HC = HID // 128
        self.S = S
        self.BC = T // S
        self.NH = NH
        self.HD = 64
        self.QT = min(1024, T)
        self.NQ = T // self.QT
        self.NTQ = self.QT // 128
        self.TB = self.QT // 512
        self.KT = S // 128
        self.QHW = min(1024, S)
        self.NQH = S // self.QHW
        self.emit = emit
        self.taps = taps


class Sem:
    def __init__(self, nc, st, name, prime=0):
        self.h = st.enter_context(nc.semaphore(name))
        self.n = 0
        self.prime = prime
        if prime:
            nc.sync.sem_inc(self.h, prime)
            self.n = prime
    def inc(self, ins, k=1):
        ins.then_inc(self.h, k)
        self.n += k
        return ins


_RC_N = [0]
_NM_N = [0]
def _nm(n):
    _NM_N[0] += 1
    return f"{n}_{_NM_N[0]}"
class RC:
    def __init__(self, eng, name, init):
        self.eng = eng
        _RC_N[0] += 1
        self.r = eng.alloc_register(f"{name}_{_RC_N[0]}")
        eng.reg_mov(self.r, init)
    def use(self, step):
        r = self.r
        return r, step
    def wait(self, sem, step):
        self.eng.wait_ge(sem, self.r)
        if step:
            self.eng.reg_add(self.r, self.r, step)
    def bump(self, k):
        if k:
            self.eng.reg_add(self.r, self.r, k)


def build_program(cfg: Cfg):
    nc = bass.Bass("TRN2", target_bir_lowering=False, debug=False)
    c = cfg
    st = ExitStack()

    dt = nc.dram_tensor
    x_in = dt("x_in", [c.T, D], F32, kind="ExternalInput").ap()
    qkv_wT = dt("qkv_wT", [D, 3 * c.NH * c.HD], F32, kind="ExternalInput").ap()
    out_wT = dt("out_wT", [c.NH * c.HD, D], F32, kind="ExternalInput").ap()
    bias_p = dt("bias_p", [1, c.NH], F32, kind="ExternalInput").ap()
    bias_a = dt("bias_a", [1, c.NH], F32, kind="ExternalInput").ap()
    rfreqs = dt("rfreqs", [1, c.HD // 2], F32, kind="ExternalInput").ap()
    w_router = dt("w_router", [D, c.E], F32, kind="ExternalInput").ap()
    w1 = dt("w1", [c.E, c.HC, 128, FC * 128], F32, kind="ExternalInput").ap()
    b1l = dt("b1l", [128, c.E, c.HC], F32, kind="ExternalInput").ap()
    w2 = dt("w2", [c.E, c.HID, D], F32, kind="ExternalInput").ap()
    b2 = dt("b2", [c.E, D], F32, kind="ExternalInput").ap()
    iotab = dt("iotab", [128, c.S], F32, kind="ExternalInput").ap()
    iota_row = dt("iota_row", [1, max(c.S, 128)], F32, kind="ExternalInput").ap()
    eyeb_in = dt("eyeb", [128, 128], BF16, kind="ExternalInput").ap()
    eyef_in = dt("eyef", [128, 128], F32, kind="ExternalInput").ap()
    y_out = dt("y_out", [c.T, D], F32, kind="ExternalOutput").ap()

    def scratch(name, shape, dtp):
        kind = "ExternalOutput" if name in c.taps else "Internal"
        return dt(name, shape, dtp, kind=kind).ap()
    x1d = scratch("x1", [c.T, D], F32)
    x2d = scratch("x2", [c.T, D], F32)
    aod = scratch("ao", [c.BC, c.NH * c.HD, c.S], BF16)
    qfd = scratch("qf", [c.BC, c.NH, 64, c.S], BF16)
    kfd = scratch("kf", [c.BC, c.NH, 64, c.S], BF16)

    sb = lambda name, shape, dtp: st.enter_context(nc.sbuf_tensor(name, shape, dtp))
    eyeb = sb("eyeb_s", [128, 128], BF16)
    eyef = sb("eyef_s", [128, 128], F32)
    iotac_s = sb("iotac_s", [128, 1], F32)
    b1sb = sb("b1sb", [128, c.E * c.HC], F32)
    b2sb = sb("b2sb", [c.E, D], BF16)
    wrt_sb = sb("wrt_sb", [128, FC * c.E], BF16)
    pa_sb = sb("pa_sb", [1, 2 * c.NH], F32)
    freq_sb = sb("freq_sb", [1, c.HD // 2], F32)
    NT_S = c.S // 128
    cs_sb = sb("cs_sb", [128, NT_S * c.HD], F32)
    junk = sb("junk", [128, 512], F32)
    onesb = sb("onesb", [1, 128], BF16)
    epscol = sb("epscol", [128, 1], F32)

    S_ = lambda name, prime=0: Sem(nc, st, name, prime)
    sPre = S_("sPre")
    sPreG = S_("sPreG")
    sPos = S_("sPos")
    sV = S_("sV")
    sS = S_("sS")
    sT = S_("sT")
    sX = [S_("sX0"), S_("sX1")]
    sX2 = [S_("sX2a"), S_("sX2b")]
    NB = 4
    sW1 = [S_(f"sW1_{i}") for i in range(NB)]
    sW2 = S_("sW2")
    sB1 = [S_("sB1a"), S_("sB1b")]
    sWc = [S_("sWca"), S_("sWcb")]
    sMM = S_("sMM", prime=64)
    sG = S_("sG", prime=64)
    sF2 = S_("sF2", prime=64)
    sAcc = S_("sAcc", prime=64)
    sOut = S_("sOut")
    sEo = [S_("sEoA"), S_("sEoB")]

    q = nc.sync
    g = nc.gpsimd
    t = nc.tensor
    v = nc.vector
    s = nc.scalar
    dma_sems = [sPre, sPreG, sPos, sX[0], sX[1], sX2[0], sX2[1], sW1[0], sW1[1],
                sW1[2], sW1[3], sW2, sB1[0], sB1[1], sWc[0], sWc[1], sEo[0], sEo[1],
                sOut]

    def sublayer_entry():
        for eng in (q, g, t, v, s):
            for sm in dma_sems:
                eng.wait_ge(sm.h, sm.n)
            eng.wait_ge(sV.h, sV.n)
            eng.wait_ge(sS.h, sS.n)
            eng.wait_ge(sT.h, sT.n)

    sPre.inc(q.dma_start(out=eyeb[:], in_=eyeb_in), 16)
    sPre.inc(q.dma_start(out=eyef[:], in_=eyef_in), 16)
    sPre.inc(q.dma_start(out=iotac_s[:],
                         in_=AP(iota_row.tensor, 0, [[1, 128], [1, 1]])), 16)
    sPre.inc(q.dma_start(out=b1sb[:].rearrange("p (e h) -> p e h", e=c.E), in_=b1l), 16)
    sPre.inc(q.dma_start(out=pa_sb[:, 0:c.NH], in_=bias_p), 16)
    sPre.inc(q.dma_start(out=pa_sb[:, c.NH:2 * c.NH], in_=bias_a), 16)
    sPre.inc(q.dma_start(out=freq_sb[:], in_=rfreqs), 16)
    sPreG.inc(g.dma_start(out=b2sb[:], in_=b2), 16)
    sPreG.inc(g.dma_start(out=wrt_sb[:].rearrange("p (f e) -> p f e", f=FC),
                          in_=w_router.rearrange("(f p) e -> p f e", p=128)), 16)

    v.wait_ge(sPre.h, sPre.n)
    v.wait_ge(sPreG.h, sPreG.n)
    sV.inc(v.tensor_scalar(out=pa_sb[:], in0=pa_sb[:], scalar1=0.01, scalar2=None,
                           op0=ALU.max))
    sV.inc(v.memset(onesb[:], 1.0))
    sV.inc(v.memset(epscol[:], 1e-5))

    HD2 = c.HD // 2
    with nc.psum_tensor("pcs", [128, 512], F32) as pcs, \
         nc.sbuf_tensor("posrow", [1, max(c.S, 128)], F32) as posrow, \
         nc.sbuf_tensor("angf", [128, c.HD], F32) as angf, \
         nc.sbuf_tensor("angi", [128, c.HD], I32) as angi, \
         nc.sbuf_tensor("angr", [128, c.HD], F32) as angr:
        sPos.inc(q.dma_start(out=posrow[:], in_=iota_row), 16)
        for tt in range(NT_S):
            t.wait_ge(sPre.h, sPre.n)
            t.wait_ge(sPos.h, sPos.n)
            t.wait_ge(sV.h, sV.n)
            sT.inc(t.matmul(pcs[:, 0:HD2], posrow[:, tt * 128:(tt + 1) * 128],
                            freq_sb[:], start=True, stop=True))
            v.wait_ge(sT.h, sT.n)
            v.wait_ge(sS.h, sS.n)
            sV.inc(v.tensor_scalar(out=angf[:, 0:HD2], in0=pcs[:, 0:HD2],
                                   scalar1=float(np.pi / 2), scalar2=None, op0=ALU.add))
            sV.inc(v.tensor_copy(angf[:, HD2:c.HD], pcs[:, 0:HD2]))
            v.wait_ge(sV.h, sV.n)
            sV.inc(v.tensor_scalar(out=angr[:], in0=angf[:],
                                   scalar1=float(1.0 / (2 * np.pi)), scalar2=None,
                                   op0=ALU.mult))
            v.wait_ge(sV.h, sV.n)
            sV.inc(v.tensor_copy(angi[:], angr[:]))
            v.wait_ge(sV.h, sV.n)
            sV.inc(v.tensor_copy(angr[:], angi[:]))
            v.wait_ge(sV.h, sV.n)
            sV.inc(v.scalar_tensor_tensor(out=angf[:], in0=angr[:],
                                          scalar=float(-2 * np.pi), in1=angf[:],
                                          op0=ALU.mult, op1=ALU.add))
            v.wait_ge(sV.h, sV.n)
            sV.inc(v.tensor_scalar(out=angr[:], in0=angf[:], scalar1=float(np.pi),
                                   scalar2=None, op0=ALU.is_gt))
            v.wait_ge(sV.h, sV.n)
            sV.inc(v.scalar_tensor_tensor(out=angf[:], in0=angr[:],
                                          scalar=float(-2 * np.pi), in1=angf[:],
                                          op0=ALU.mult, op1=ALU.add))
            s.wait_ge(sV.h, sV.n)
            sS.inc(s.activation(out=cs_sb[:, tt * c.HD:(tt + 1) * c.HD], in_=angf[:],
                                func=AF.Sin))

    def ffn_sublayer(x_src, x_dst):
        E, HC, QT, NTQ, TB = c.E, c.HC, c.QT, c.NTQ, c.TB
        assert (E * HC) % NB == 0
        fst = ExitStack()
        fsb = lambda n, sh, d: fst.enter_context(nc.sbuf_tensor(_nm(n), sh, d))
        x_sb = fsb("x_sb", [128, 2 * D], F32)
        xhat_tm = fsb("xhat_tm", [128, D], BF16)
        xhat_fm = fsb("xhat_fm", [128, FC * QT], BF16)
        red = fsb("red", [128, 8], F32)
        w1buf = fsb("w1buf", [128, NB * D], BF16)
        w2_sb = fsb("w2_sb", [128, HC * D], BF16)
        g_sb = fsb("g_sb", [128, HC * QT], BF16)
        acc = fsb("acc", [128, NTQ * D], F32)
        wq_sb = fsb("wq_sb", [128, NTQ * E], F32)
        wtokT = fsb("wtokT", [c.E, QT], BF16)
        pexp = fsb("pexp", [128, E], F32)
        probs = fsb("probs", [128, E], F32)
        msk = fsb("msk", [128, E], F32)
        p2t = fsb("p2t", [128, E], F32)
        b1cur = fsb("b1cur", [128, 2 * HC], F32)
        wcur = fsb("wcur", [128, 2 * NTQ], F32)
        wq2 = fsb("wq2", [128, E * NTQ], F32)
        pst = ExitStack()
        ph0 = pst.enter_context(nc.psum_tensor(_nm("ph0"), [128, 512], F32))
        ph1 = pst.enter_context(nc.psum_tensor(_nm("ph1"), [128, 512], F32))
        po = pst.enter_context(nc.psum_tensor(_nm("po"), [128, 4 * 512], F32))
        pr = pst.enter_context(nc.psum_tensor(_nm("pr"), [128, 512], F32))
        pt = pst.enter_context(nc.psum_tensor(_nm("pt"), [128, 1024], BF16))
        phs = [ph0, ph1]

        sublayer_entry()
        for qq in range(c.NQ):
            tok0 = qq * QT
            v.wait_ge(sMM.h, sMM.n)
            for tt in range(NTQ):
                sl = tt % 2
                r0 = (tt % 2) * 4
                q.wait_ge(sV.h, sV.n)
                q.wait_ge(sS.h, sS.n)
                sX[sl].inc(q.dma_start(
                    out=x_sb[:, sl * D:(sl + 1) * D],
                    in_=x_src[tok0 + tt * 128: tok0 + (tt + 1) * 128, :]), 16)
                v.wait_ge(sX[sl].h, sX[sl].n)
                v.wait_ge(sT.h, sT.n)
                xs = x_sb[:, sl * D:(sl + 1) * D]
                sV.inc(v.reduce_sum(out=red[:, r0:r0 + 1], in_=xs, axis=AX.X, negate=True))
                v.wait_ge(sV.h, sV.n)
                sV.inc(v.tensor_scalar(out=red[:, r0 + 1:r0 + 2], in0=red[:, r0:r0 + 1],
                                       scalar1=float(1.0 / D), scalar2=None, op0=ALU.mult))
                v.wait_ge(sV.h, sV.n)
                sV.inc(v.tensor_scalar(out=xs, in0=xs, scalar1=red[:, r0 + 1:r0 + 2],
                                       scalar2=None, op0=ALU.add))
                s.wait_ge(sV.h, sV.n)
                sS.inc(s.activation(out=junk[:], in_=xs[:, 0:512], func=AF.Square,
                                    accum_out=red[:, r0 + 2:r0 + 3]))
                s.wait_ge(sS.h, sS.n)
                sS.inc(s.activation(out=junk[:], in_=xs[:, 512:1024], func=AF.Square,
                                    accum_out=red[:, r0 + 3:r0 + 4]))
                v.wait_ge(sS.h, sS.n)
                sV.inc(v.tensor_tensor(out=red[:, r0 + 2:r0 + 3], in0=red[:, r0 + 2:r0 + 3],
                                       in1=red[:, r0 + 3:r0 + 4], op=ALU.add))
                v.wait_ge(sV.h, sV.n)
                sV.inc(v.tensor_scalar(out=red[:, r0 + 3:r0 + 4], in0=red[:, r0 + 2:r0 + 3],
                                       scalar1=float(1.0 / D), scalar2=1e-5,
                                       op0=ALU.mult, op1=ALU.add))
                s.wait_ge(sV.h, sV.n)
                sS.inc(s.activation(out=red[:, r0 + 2:r0 + 3], in_=red[:, r0 + 3:r0 + 4],
                                    func=AF.Ln))
                s.wait_ge(sS.h, sS.n)
                sS.inc(s.activation(out=red[:, r0 + 3:r0 + 4], in_=red[:, r0 + 2:r0 + 3],
                                    func=AF.Exp, scale=-0.5))
                v.wait_ge(sS.h, sS.n)
                sV.inc(v.tensor_scalar(out=xhat_tm[:, 0:D], in0=xs,
                                       scalar1=red[:, r0 + 3:r0 + 4], scalar2=None,
                                       op0=ALU.mult))
                t.wait_ge(sV.h, sV.n)
                for j in range(FC):
                    ins = t.transpose(pt[:, j * 128:(j + 1) * 128],
                                      xhat_tm[:, j * 128:(j + 1) * 128],
                                      eyeb[:])
                sT.inc(ins)
                v.wait_ge(sT.h, sT.n)
                sV.inc(v.tensor_copy(
                    xhat_fm[:].rearrange("p (f w) -> p f w", f=FC)[:, :, tt * 128:(tt + 1) * 128],
                    pt[:].rearrange("p (f w) -> p f w", f=FC)))
                t.wait_ge(sV.h, sV.n)
                for j in range(FC):
                    ins = t.matmul(pr[:, 0:E],
                                   xhat_fm[:, j * QT + tt * 128: j * QT + (tt + 1) * 128],
                                   wrt_sb[:, j * E:(j + 1) * E],
                                   start=(j == 0), stop=(j == FC - 1))
                sT.inc(ins)
                v.wait_ge(sT.h, sT.n)
                sV.inc(v.reduce_max(out=red[:, r0 + 3:r0 + 4], in_=pr[:, 0:E], axis=AX.X,
                                    negate=True))
                s.wait_ge(sV.h, sV.n)
                sS.inc(s.activation(out=pexp[:], in_=pr[:, 0:E], func=AF.Exp,
                                    bias=red[:, r0 + 3:r0 + 4], accum_out=red[:, r0:r0 + 1]))
                v.wait_ge(sS.h, sS.n)
                sV.inc(v.reciprocal(out=red[:, r0 + 1:r0 + 2], in_=red[:, r0:r0 + 1]))
                v.wait_ge(sV.h, sV.n)
                sV.inc(v.tensor_scalar(out=probs[:], in0=pexp[:],
                                       scalar1=red[:, r0 + 1:r0 + 2], scalar2=None,
                                       op0=ALU.mult))
                v.wait_ge(sV.h, sV.n)
                sV.inc(v.reduce_max(out=red[:, r0:r0 + 1], in_=probs[:], axis=AX.X))
                v.wait_ge(sV.h, sV.n)
                sV.inc(v.tensor_scalar(out=msk[:], in0=probs[:], scalar1=red[:, r0:r0 + 1],
                                       scalar2=None, op0=ALU.is_ge))
                v.wait_ge(sV.h, sV.n)
                sV.inc(v.scalar_tensor_tensor(out=p2t[:], in0=msk[:], scalar=-1e30,
                                              in1=probs[:], op0=ALU.mult, op1=ALU.add))
                v.wait_ge(sV.h, sV.n)
                sV.inc(v.reduce_max(out=red[:, r0 + 1:r0 + 2], in_=p2t[:], axis=AX.X))
                v.wait_ge(sV.h, sV.n)
                sV.inc(v.tensor_scalar(out=msk[:], in0=probs[:],
                                       scalar1=red[:, r0 + 1:r0 + 2], scalar2=None,
                                       op0=ALU.is_ge))
                v.wait_ge(sV.h, sV.n)
                sV.inc(v.scalar_tensor_tensor(out=wq_sb[:, tt * E:(tt + 1) * E],
                                              in0=probs[:], scalar=0.5, in1=msk[:],
                                              op0=ALU.mult, op1=ALU.mult))
                v.wait_ge(sV.h, sV.n)
                sV.inc(v.tensor_copy(
                    AP(wq2[:].tensor, tt, [[E * NTQ, 128], [NTQ, E]]),
                    wq_sb[:, tt * E:(tt + 1) * E]))
                t.wait_ge(sV.h, sV.n)
                sT.inc(t.transpose(pr[0:E, 128:256], wq_sb[:, tt * E:(tt + 1) * E], eyef[:]))
                v.wait_ge(sT.h, sT.n)
                sV.inc(v.tensor_copy(wtokT[:, tt * 128:(tt + 1) * 128], pr[0:E, 128:256]))
            v.wait_ge(sV.h, sV.n)
            sV.inc(v.memset(acc[:], 0.0))

            f2base = sF2.n
            gbase = sG.n
            mmbase = sMM.n
            accbase = sAcc.n
            w1base = [sW1[j].n for j in range(NB)]
            w2base = sW2.n
            for e in range(E):
                sl_e = e % 2
                g.wait_ge(sG.h, gbase + max(0, (e - 1)) * (HC * TB) if e >= 2 else 0)
                sB1[sl_e].inc(g.dma_start(out=b1cur[:, sl_e * HC:(sl_e + 1) * HC],
                                          in_=b1l[:, e, :]), 16)
                g.wait_ge(sV.h, sV.n)
                g.wait_ge(sAcc.h, accbase + max(0, (e - 1)) * (NTQ * 2) if e >= 2 else 0)
                sWc[sl_e].inc(g.dma_start(out=wcur[:, sl_e * NTQ:(sl_e + 1) * NTQ],
                                          in_=wq2[:, e * NTQ:(e + 1) * NTQ]), 16)
                for hc in range(HC):
                    ci = e * HC + hc
                    sl = ci % NB
                    g.wait_ge(sG.h, gbase + (ci - NB + 1) * TB)
                    sW1[sl].inc(g.dma_start(
                        out=w1buf[:, sl * D:(sl + 1) * D],
                        in_=w1[e, hc]), 16)
                for ohh in range(2):
                    prev = (e - 1) * NTQ * 2 + (ohh + 1) * NTQ if e >= 1 else 0
                    g.wait_ge(sF2.h, f2base + prev)
                    sW2.inc(g.dma_start(
                        out=w2_sb[:, ohh * HC * 512:(ohh + 1) * HC * 512]
                            .rearrange("p (h w) -> p h w", h=HC),
                        in_=w2[e].rearrange("(h p) d -> p h d", p=128)
                            [:, :, ohh * 512:(ohh + 1) * 512]), 16)

            rW1 = [RC(t, f"rW1_{j}", w1base[j] + 16) for j in range(NB)]
            rG = RC(t, "rG", gbase - 1)
            rGE = RC(t, "rGE", gbase + HC * TB)
            rW2 = RC(t, "rW2", w2base + 16)
            rAc = RC(t, "rAc", accbase - 3)
            with t.Fori(0, E) as ei:
                for gi in range(HC // NB):
                    for j in range(NB):
                        rW1[j].wait(sW1[j].h, 16)
                        for tb in range(TB):
                            k = (gi * NB + j) * TB + tb
                            rG.wait(sG.h, 1)
                            ps = phs[k % 2]
                            for fc_ in range(FC):
                                ins = t.matmul(
                                    ps[:],
                                    w1buf[:, j * D + fc_ * 128: j * D + (fc_ + 1) * 128],
                                    xhat_fm[:, fc_ * QT + tb * 512: fc_ * QT + tb * 512 + 512],
                                    start=(fc_ == 0), stop=(fc_ == FC - 1))
                            ins.then_inc(sMM.h, 1)
                rGE.wait(sG.h, HC * TB)
                for oh in range(2):
                    rW2.wait(sW2.h, 16)
                    for ts in range(NTQ):
                        g2 = oh * NTQ + ts
                        rAc.wait(sAcc.h, 1)
                        pob = po[:, (g2 % 4) * 512:(g2 % 4) * 512 + 512]
                        for hc in range(HC):
                            ins = t.matmul(
                                pob,
                                g_sb[:, hc * QT + ts * 128: hc * QT + (ts + 1) * 128],
                                w2_sb[:, oh * HC * 512 + hc * 512:
                                      oh * HC * 512 + (hc + 1) * 512],
                                start=(hc == 0), stop=(hc == HC - 1))
                        ins.then_inc(sF2.h, 1)
            for j in range(NB):
                t.free_register(rW1[j].r)
            for rc in (rG, rGE, rW2, rAc):
                t.free_register(rc.r)
            sMM.n = mmbase + E * HC * TB
            sF2.n = f2base + E * NTQ * 2

            rF2s = RC(s, "rF2s", f2base)
            rMMs = RC(s, "rMMs", mmbase + 1)
            rB1w = [RC(s, "rB1w0", sB1[0].n - (E // 2 - 1) * 16),
                    RC(s, "rB1w1", sB1[1].n - (E // 2 - 1) * 16)]
            with s.Fori(0, E // 2) as ei:
                for sl_e in range(2):
                    rF2s.wait(sF2.h, NTQ * 2)
                    rB1w[sl_e].wait(sB1[sl_e].h, 16)
                    for gi in range(HC // NB):
                        for j in range(NB):
                            hcq = gi * NB + j
                            for tb in range(TB):
                                k = hcq * TB + tb
                                rMMs.wait(sMM.h, 1)
                                ps = phs[k % 2]
                                dst = AP(g_sb[:].tensor, hcq * QT + tb * 512,
                                         [[HC * QT, 128], [1, 512]])
                                s.activation(out=dst, in_=ps[:], func=AF.Gelu_apprx_tanh,
                                             bias=b1cur[:, sl_e * HC + hcq: sl_e * HC + hcq + 1]
                                             ).then_inc(sG.h, 1)
            for rc in (rF2s, rMMs, rB1w[0], rB1w[1]):
                s.free_register(rc.r)
            sG.n = gbase + E * HC * TB

            v.wait_ge(sV.h, sV.n)
            rF2v = RC(v, "rF2v", f2base + 1)
            rAcv = RC(v, "rAcv", accbase - (NTQ * 2) + 1)
            rWcw = [RC(v, "rWcw0", sWc[0].n - (E // 2 - 1) * 16),
                    RC(v, "rWcw1", sWc[1].n - (E // 2 - 1) * 16)]
            with v.Fori(0, E // 2) as ei:
                for sl_e in range(2):
                    rWcw[sl_e].wait(sWc[sl_e].h, 16)
                    for oh in range(2):
                        for ts in range(NTQ):
                            rF2v.wait(sF2.h, 1)
                            rAcv.wait(sAcc.h, 1)
                            g2 = oh * NTQ + ts
                            pob = po[:, (g2 % 4) * 512:(g2 % 4) * 512 + 512]
                            at = acc[:, ts * D + oh * 512: ts * D + oh * 512 + 512]
                            wptr = wcur[:, sl_e * NTQ + ts: sl_e * NTQ + ts + 1]
                            v.scalar_tensor_tensor(out=at, in0=pob, scalar=wptr, in1=at,
                                                   op0=ALU.mult, op1=ALU.add).then_inc(sAcc.h, 1)
            for rc in (rF2v, rAcv, rWcw[0], rWcw[1]):
                v.free_register(rc.r)
            sAcc.n = accbase + E * NTQ * 2

            eobase = [sEo[0].n, sEo[1].n]
            for ts in range(NTQ):
                sl = ts % 2
                q.wait_ge(sV.h, sV.n)
                q.wait_ge(sS.h, sS.n)
                if ts >= 2:
                    q.wait_ge(sEo[sl].h, eobase[sl] + (ts // 2) * 16)
                sX2[sl].inc(q.dma_start(
                    out=x_sb[:, sl * D:(sl + 1) * D],
                    in_=x_src[tok0 + ts * 128: tok0 + (ts + 1) * 128, :]), 16)
                t.wait_ge(sPreG.h, sPreG.n)
                t.wait_ge(sAcc.h, sAcc.n)
                for oh in range(2):
                    t.wait_ge(sV.h, sV.n)
                    sT.inc(t.matmul(pr[:, 0:512], wtokT[:, ts * 128:(ts + 1) * 128],
                                    b2sb[:, oh * 512:(oh + 1) * 512], start=True, stop=True))
                    v.wait_ge(sT.h, sT.n)
                    v.wait_ge(sX2[sl].h, sX2[sl].n)
                    et = x_sb[:, sl * D + oh * 512: sl * D + oh * 512 + 512]
                    sV.inc(v.tensor_tensor(out=et, in0=et, in1=pr[:, 0:512], op=ALU.add))
                    v.wait_ge(sV.h, sV.n)
                    at = acc[:, ts * D + oh * 512: ts * D + oh * 512 + 512]
                    sV.inc(v.tensor_tensor(out=et, in0=et, in1=at, op=ALU.add))
                q.wait_ge(sV.h, sV.n)
                sEo[sl].inc(q.dma_start(out=x_dst[tok0 + ts * 128: tok0 + (ts + 1) * 128, :],
                                        in_=x_sb[:, sl * D:(sl + 1) * D]), 16)
        pst.close()
        fst.close()

    def attn_sublayer(x_src, x_dst):
        _emit_attention(nc, c, x_src, x_dst, dict(
            qkv_wT=qkv_wT, out_wT=out_wT, aod=aod, qfd=qfd, kfd=kfd, iotab=iotab,
            eyeb=eyeb, eyef=eyef, iotac_s=iotac_s, pa_sb=pa_sb, epscol=epscol,
            cs_sb=cs_sb, onesb=onesb, junk=junk, sPos=sPos,
            dma_sems=dma_sems, sublayer_entry=sublayer_entry,
            sems=dict(sPre=sPre, sV=sV, sS=sS, sT=sT, sX=sX, sX2=sX2, sOut=sOut,
                      sEo=sEo),
            S_=S_))

    if 1 in c.emit:
        dst1 = x1d if (2 in c.emit or 3 in c.emit or "x1" in c.taps) else y_out
        ffn_sublayer(x_in, dst1)
    if 2 in c.emit:
        src2 = x1d if 1 in c.emit else x_in
        dst2 = x2d if (3 in c.emit or "x2" in c.taps) else y_out
        attn_sublayer(src2, dst2)
    if 3 in c.emit:
        src3 = x2d if 2 in c.emit else (x1d if 1 in c.emit else x_in)
        ffn_sublayer(src3, y_out)

    q.wait_ge(sOut.h, sOut.n)
    q.wait_ge(sEo[0].h, sEo[0].n)
    q.wait_ge(sEo[1].h, sEo[1].n)
    st.close()
    return nc


def _emit_attention(nc, c, x_src, x_dst, env):
    _attn_emit(nc, c, x_src, x_dst, env)


def _attn_emit(nc, c, x_src, x_dst, env):
    qkv_wT, out_wT = env["qkv_wT"], env["out_wT"]
    aod, qfd, kfd, iotab = env["aod"], env["qfd"], env["kfd"], env["iotab"]
    eyeb, eyef, iotac_s, pa_sb = env["eyeb"], env["eyef"], env["iotac_s"], env["pa_sb"]
    cs_sb, onesb, junk = env["cs_sb"], env["onesb"], env["junk"]
    epscol = env["epscol"]
    sems = env["sems"]
    sPre, sV, sS, sT = sems["sPre"], sems["sV"], sems["sS"], sems["sT"]
    sX, sEo = sems["sX"], sems["sEo"]
    S_ = env["S_"]

    q = nc.sync
    g = nc.gpsimd
    t = nc.tensor
    v = nc.vector
    s = nc.scalar

    S, NH, KT, BC = c.S, c.NH, c.KT, c.BC
    QHW, NQH = c.QHW, c.NQH
    NO = 3 * NH * 64 // 512
    NB8 = NH // 8

    sQV = S_("sQV")
    sSp = S_("sSp")
    sQF = [S_("sQF0"), S_("sQF1")]
    sKF = [S_("sKF0"), S_("sKF1")]
    sAO = [S_("sAOa"), S_("sAOb")]
    sAF = [S_("sAFa"), S_("sAFb")]
    sOW = S_("sOW")
    sGd = S_("sGd")
    env["dma_sems"].extend([sQV, sSp, sQF[0], sQF[1], sKF[0], sKF[1],
                            sAO[0], sAO[1], sAF[0], sAF[1], sOW])

    ast = ExitStack()
    asb = lambda n, sh, d: ast.enter_context(nc.sbuf_tensor(n, sh, d))
    iotab_s = asb("iotab_s", [128, S], F32)
    iotabn_s = asb("iotabn_s", [128, S], F32)
    v_sb2 = asb("v_sb2", [128, BC * KT * (NH * 65)], BF16)
    pacols = asb("pacols", [128, 2 * NH], F32)
    onesf = asb("onesf", [1, 128], F32)
    env["sublayer_entry"]()
    sPre.inc(q.dma_start(out=iotab_s[:], in_=iotab), 16)

    v.wait_ge(sPre.h, sPre.n)
    sV.inc(v.memset(onesf[:], 1.0))
    v.wait_ge(sV.h, sV.n)
    sV.inc(v.tensor_scalar(out=iotabn_s[:], in0=iotab_s[:], scalar1=-1.0,
                           scalar2=None, op0=ALU.mult))

    with nc.psum_tensor("ppa", [128, 512], F32) as ppa:
        for h in range(NH):
            t.wait_ge(sV.h, sV.n)
            t.wait_ge(sPre.h, sPre.n)
            sT.inc(t.matmul(ppa[:, 0:1], onesf[:], pa_sb[0:1, h:h + 1],
                            start=True, stop=True))
            sT.inc(t.matmul(ppa[:, 1:2], onesf[:], pa_sb[0:1, NH + h:NH + h + 1],
                            start=True, stop=True))
            v.wait_ge(sT.h, sT.n)
            sV.inc(v.tensor_scalar(out=pacols[:, 2 * h:2 * h + 1], in0=ppa[:, 0:1],
                                   scalar1=-8.0, scalar2=None, op0=ALU.mult))
            sV.inc(v.tensor_copy(pacols[:, 2 * h + 1:2 * h + 2], ppa[:, 1:2]))

    mEndA1 = {}
    with ExitStack() as a1st:
        a1sb = lambda n, sh, d: a1st.enter_context(nc.sbuf_tensor(n, sh, d))
        x_sb = a1sb("x_sb_a", [128, 2 * D], F32)
        junka = a1sb("junka", [128, D], F32)
        xhat_tm = a1sb("xhat_tm_a", [128, D], BF16)
        xt_fm = a1sb("xt_fm", [128, 2 * D], BF16)
        red = a1sb("red_a", [128, 12], F32)
        qkvw_sb = a1sb("qkvw_sb", [128, NO * FC * 512], BF16)
        csrep = a1sb("csrep", [128, KT * 512], F32)
        ropeA = a1sb("ropeA", [128, 256], F32)
        ropeB = a1sb("ropeB", [128, 256], F32)
        qk_tm = a1sb("qk_tm", [128, 512], BF16)
        qstage = a1sb("qstage", [128, 2048], BF16)
        kstage = a1sb("kstage", [128, 2048], BF16)
        with nc.psum_tensor("pq", [128, 2 * 512], F32) as pq, \
             nc.psum_tensor("pt2", [128, 1024], BF16) as pt2:

            g.wait_ge(sPre.h, sPre.n)
            for ob in range(NO):
                sQV.inc(g.dma_start(
                    out=qkvw_sb[:, ob * FC * 512:(ob + 1) * FC * 512]
                        .rearrange("p (f w) -> p f w", f=FC),
                    in_=qkv_wT.rearrange("(f p) o -> p f o", p=128)
                        [:, :, ob * 512:(ob + 1) * 512]), 16)
            v.wait_ge(sS.h, sS.n)
            v.wait_ge(sV.h, sV.n)
            for tt in range(KT):
                for rr in range(8):
                    sV.inc(v.tensor_copy(csrep[:, tt * 512 + rr * 32: tt * 512 + (rr + 1) * 32],
                                         cs_sb[:, tt * 64: tt * 64 + 32]))
                    sV.inc(v.tensor_copy(
                        csrep[:, tt * 512 + 256 + rr * 32: tt * 512 + 256 + (rr + 1) * 32],
                        cs_sb[:, tt * 64 + 32: tt * 64 + 64]))
            mCsrep = sV.n

            mLNx = [0, 0]
            mA0tr = 0
            mXhat = [0, 0]
            mXuse = [0, 0]
            mPQslot = {0: (None, 0), 1: (None, 0)}
            mRope = 0
            mQKtr = 0
            mStage = 0
            mSpillTot = 0
            for seq in range(BC):
                base_tok = seq * S
                for tt in range(KT):
                    sl = tt % 2
                    r0 = 6 * sl
                    q.wait_ge(sV.h, mLNx[sl])
                    sX[sl].inc(q.dma_start(
                        out=x_sb[:, sl * D:(sl + 1) * D],
                        in_=x_src[base_tok + tt * 128: base_tok + (tt + 1) * 128, :]), 16)
                    xs = x_sb[:, sl * D:(sl + 1) * D]
                    s.wait_ge(sX[sl].h, sX[sl].n)
                    sS.inc(s.activation(out=junka[:], in_=xs, func=AF.Copy,
                                        accum_out=red[:, r0:r0 + 1]))
                    s.wait_ge(sS.h, sS.n)
                    sS.inc(s.activation(out=red[:, r0 + 1:r0 + 2], in_=red[:, r0:r0 + 1],
                                        func=AF.Copy, scale=float(-1.0 / D)))
                    s.wait_ge(sS.h, sS.n)
                    sS.inc(s.activation(out=junka[:], in_=xs, func=AF.Square,
                                        bias=red[:, r0 + 1:r0 + 2],
                                        accum_out=red[:, r0 + 2:r0 + 3]))
                    s.wait_ge(sS.h, sS.n)
                    sS.inc(s.activation(out=red[:, r0 + 3:r0 + 4], in_=red[:, r0 + 2:r0 + 3],
                                        func=AF.Ln, scale=float(1.0 / D),
                                        bias=epscol[:, 0:1]))
                    s.wait_ge(sS.h, sS.n)
                    sS.inc(s.activation(out=red[:, r0 + 4:r0 + 5], in_=red[:, r0 + 3:r0 + 4],
                                        func=AF.Exp, scale=-0.5))
                    v.wait_ge(sS.h, sS.n)
                    v.wait_ge(sT.h, mA0tr)
                    sV.inc(v.tensor_scalar(out=xhat_tm[:], in0=xs,
                                           scalar1=red[:, r0 + 1:r0 + 2],
                                           scalar2=red[:, r0 + 4:r0 + 5],
                                           op0=ALU.add, op1=ALU.mult))
                    mLNx[sl] = sV.n
                    t.wait_ge(sV.h, mLNx[sl])
                    t.wait_ge(sV.h, mStage)
                    for j in range(FC):
                        ins = t.transpose(pt2[:, j * 128:(j + 1) * 128],
                                          xhat_tm[:, j * 128:(j + 1) * 128], eyeb[:])
                    sT.inc(ins)
                    mA0tr = sT.n
                    v.wait_ge(sT.h, mA0tr)
                    v.wait_ge(sT.h, mXuse[sl])
                    sV.inc(v.tensor_copy(xt_fm[:, sl * D:(sl + 1) * D], pt2[:]))
                    mXhat[sl] = sV.n
                    for ob in range(NO):
                        pqs = pq[:, (ob % 2) * 512:(ob % 2) * 512 + 512]
                        t.wait_ge(sQV.h, sQV.n)
                        t.wait_ge(sV.h, mXhat[sl])
                        sem_pq, val_pq = mPQslot[ob % 2]
                        if sem_pq is not None:
                            t.wait_ge(sem_pq, val_pq)
                        for j in range(FC):
                            ins = t.matmul(
                                pqs,
                                xt_fm[:, sl * D + j * 128: sl * D + (j + 1) * 128],
                                qkvw_sb[:, ob * FC * 512 + j * 512:
                                        ob * FC * 512 + (j + 1) * 512],
                                start=(j == 0), stop=(j == FC - 1))
                        sT.inc(ins)
                        mMM = sT.n
                        if ob == NO - 1:
                            mXuse[sl] = sT.n
                        if ob < 2 * NB8:
                            is_q = ob < NB8
                            blk = ob if is_q else ob - NB8
                            pv = pqs.rearrange("p (h two d) -> p h two d", two=2, d=32)
                            t1v, t2v = pv[:, :, 0, :], pv[:, :, 1, :]
                            cr = csrep[:, tt * 512: tt * 512 + 256] \
                                .rearrange("p (h d) -> p h d", d=32)
                            sr = csrep[:, tt * 512 + 256: tt * 512 + 512] \
                                .rearrange("p (h d) -> p h d", d=32)
                            qv = qk_tm[:].rearrange("p (h two d) -> p h two d", two=2, d=32)
                            rA = ropeA[:].rearrange("p (h d) -> p h d", d=32)
                            rB = ropeB[:].rearrange("p (h d) -> p h d", d=32)
                            v.wait_ge(sT.h, mMM)
                            v.wait_ge(sT.h, mQKtr)
                            v.wait_ge(sV.h, mCsrep)
                            sV.inc(v.tensor_tensor(out=rA, in0=t1v, in1=cr, op=ALU.mult))
                            sV.inc(v.tensor_tensor(out=rB, in0=t2v, in1=sr, op=ALU.mult))
                            v.wait_ge(sV.h, sV.n)
                            sV.inc(v.tensor_tensor(out=qv[:, :, 0, :], in0=rA, in1=rB,
                                                   op=ALU.subtract))
                            v.wait_ge(sV.h, sV.n)
                            sV.inc(v.tensor_tensor(out=rA, in0=t1v, in1=sr, op=ALU.mult))
                            sV.inc(v.tensor_tensor(out=rB, in0=t2v, in1=cr, op=ALU.mult))
                            v.wait_ge(sV.h, sV.n)
                            sV.inc(v.tensor_tensor(out=qv[:, :, 1, :], in0=rA, in1=rB,
                                                   op=ALU.add))
                            mRope = sV.n
                            mPQslot[ob % 2] = (sV.h, sV.n)
                            stage = qstage if is_q else kstage
                            t.wait_ge(sV.h, mRope)
                            t.wait_ge(sV.h, mXhat[sl])
                            t.wait_ge(sV.h, mStage)
                            for hh in range(8):
                                ins = t.transpose(pt2[0:64, hh * 128:(hh + 1) * 128],
                                                  qk_tm[:, hh * 64:(hh + 1) * 64],
                                                  eyeb[:])
                            sT.inc(ins)
                            mQKtr = sT.n
                            v.wait_ge(sT.h, mQKtr)
                            v.wait_ge(sSp.h, mSpillTot)
                            for hh in range(8):
                                h = blk * 8 + hh
                                sV.inc(v.tensor_copy(
                                    stage[(h % 2) * 64:(h % 2) * 64 + 64,
                                          (h // 2) * 128:(h // 2) * 128 + 128],
                                    pt2[0:64, hh * 128:(hh + 1) * 128]))
                            mStage = sV.n
                        else:
                            bi = ob - 2 * NB8
                            base = (seq * KT + tt) * (NH * 65) + bi * 8 * 65
                            dst = AP(v_sb2[:].tensor, base,
                                     [[BC * KT * NH * 65, 128], [65, 8], [1, 64]])
                            s.wait_ge(sT.h, mMM)
                            sS.inc(s.activation(out=dst,
                                                in_=pqs.rearrange("p (h d) -> p h d", d=64),
                                                func=AF.Copy))
                            mPQslot[ob % 2] = (sS.h, sS.n)
                            od = AP(v_sb2[:].tensor, base + 64,
                                    [[BC * KT * NH * 65, 128], [65, 8], [1, 1]])
                            v.wait_ge(sS.h, sS.n)
                            sV.inc(v.memset(od, 1.0))
                    g.wait_ge(sV.h, mStage)
                    for stage, dd in ((qstage, qfd), (kstage, kfd)):
                        for half in range(2):
                            src = stage[half * 64:(half + 1) * 64,
                                        0:(NH // 2) * 128].rearrange(
                                "d (a w) -> d a w", w=128)
                            dstv = dd[seq].rearrange("(a two) d w -> two a d w", two=2)[half] \
                                .transpose([1, 0, 2])
                            sSp.inc(g.dma_start(
                                out=dstv[:, :, tt * 128:(tt + 1) * 128], in_=src), 16)
                    mSpillTot = sSp.n
        mEndA1 = dict(t=sT.n, v=sV.n, s=sS.n)

    mEndA2 = {}
    with ExitStack() as a2st:
        a2sb = lambda n, sh, d: a2st.enter_context(nc.sbuf_tensor(n, sh, d))
        qfb = a2sb("qfb", [64, 2 * BC * S], BF16)
        kfb = a2sb("kfb", [64, 2 * BC * S], BF16)
        kbc = a2sb("kbc", [128, 8 * QHW], F32)
        kbtB = a2sb("kbtB", [128, 2 * QHW], F32)
        sbt = a2sb("sbt", [128, 2 * BC * QHW], F32)
        attnT = a2sb("attnT", [128, 2 * BC * QHW], BF16)
        mred = a2sb("mred", [64, 8], F32)
        bsc = a2sb("bsc", [1, 4], F32)
        mcol = a2sb("mcol", [128, 4], F32)
        zsb = a2sb("zsb", [1, QHW], F32)
        aot = a2sb("aot", [64, QHW], F32)
        aobuf = a2sb("aobuf", [64, 2 * QHW], BF16)
        with nc.psum_tensor("pbig", [128, 2048], F32) as pbig, \
             nc.psum_tensor("psB", [128, 2048], F32) as psB:
            ppvs = [pbig[0:65, 0:QHW], pbig[0:65, QHW:2 * QHW]]
            for eng in (q, v, s, t):
                eng.wait_ge(sT.h, mEndA1["t"])
                eng.wait_ge(sV.h, mEndA1["v"])
                eng.wait_ge(sS.h, mEndA1["s"])
            q.wait_ge(sSp.h, sSp.n)
            lastBias = [0, 0]
            lastEvac = [0, 0]
            sbtExp = {0: 0, 1: 0}
            lastPV = {}
            mZbLast = 0
            kbtBbias = {}
            cacheRead = [0] * 8
            cacheMark = [0] * 8
            mQKlast = [(None, 0), (None, 0)]
            mDot = [0, 0]
            for h in range(NH):
                sl = h % 2
                qof = sl * BC * S
                semq, valq = mQKlast[sl]
                if semq is not None:
                    q.wait_ge(semq, valq)
                for seq in range(BC):
                    sQF[sl].inc(q.dma_start(
                        out=qfb[0:64, qof + seq * S: qof + (seq + 1) * S],
                        in_=qfd[seq, h]), 16)
                    sKF[sl].inc(q.dma_start(
                        out=kfb[0:64, qof + seq * S: qof + (seq + 1) * S],
                        in_=kfd[seq, h]), 16)
                loadQ, loadK = sQF[sl].n, sKF[sl].n
                v.wait_ge(sQF[sl].h, loadQ)
                v.wait_ge(sKF[sl].h, loadK)
                v.wait_ge(sT.h, mDot[sl])
                for seq in range(BC):
                    sV.inc(v.reduce_max(out=mred[:, sl * 4 + 2 * seq: sl * 4 + 2 * seq + 1],
                                        in_=qfb[0:64, qof + seq * S: qof + (seq + 1) * S],
                                        axis=AX.X, apply_absolute_value=True))
                    sV.inc(v.reduce_max(out=mred[:, sl * 4 + 2 * seq + 1: sl * 4 + 2 * seq + 2],
                                        in_=kfb[0:64, qof + seq * S: qof + (seq + 1) * S],
                                        axis=AX.X, apply_absolute_value=True))
                mRed = sV.n
                t.wait_ge(sV.h, mRed)
                t.wait_ge(sV.h, lastEvac[0])
                for seq in range(BC):
                    sT.inc(t.matmul(pbig[0:1, seq:seq + 1],
                                    mred[0:64, sl * 4 + 2 * seq: sl * 4 + 2 * seq + 1],
                                    mred[0:64, sl * 4 + 2 * seq + 1: sl * 4 + 2 * seq + 2],
                                    start=True, stop=True))
                mDot[sl] = sT.n
                v.wait_ge(sT.h, mDot[sl])
                sV.inc(v.tensor_copy(bsc[:, sl * 2:sl * 2 + 2], pbig[0:1, 0:2]))
                v.wait_ge(sV.h, sV.n)
                sV.inc(v.tensor_tensor(out=bsc[:, sl * 2:sl * 2 + 1],
                                       in0=bsc[:, sl * 2:sl * 2 + 1],
                                       in1=bsc[:, sl * 2 + 1:sl * 2 + 2], op=ALU.max))
                mBsc = sV.n
                t.wait_ge(sV.h, mBsc)
                sT.inc(t.matmul(pbig[:, 0:1], onesf[0:1, 0:128],
                                bsc[0:1, sl * 2:sl * 2 + 1], start=True, stop=True))
                mPm = sT.n
                v.wait_ge(sT.h, mPm)
                sV.inc(v.tensor_scalar(out=mcol[:, sl:sl + 1], in0=pbig[:, 0:1],
                                       scalar1=-0.125, scalar2=None, op0=ALU.mult))
                mMcol = sV.n

                def kb_prep(qh, kt):
                    cached = (qh == 1 and kt >= 8)
                    tocache = (qh == 0 and kt < 8)
                    if cached:
                        slot = kt - 8
                        return kbc[:, slot * QHW:(slot + 1) * QHW], cacheMark[slot]
                    if tocache:
                        dst = kbc[:, kt * QHW:(kt + 1) * QHW]
                        war = cacheRead[kt]
                    else:
                        pB = kt % 2
                        dst = kbtB[:, pB * QHW:(pB + 1) * QHW]
                        war = kbtBbias.get(pB, 0)
                    g.wait_ge(sV.h, war)
                    if qh == 0 and kt >= 8:
                        sGd.inc(g.tensor_scalar(
                            out=dst, in0=iotabn_s[:, 0:QHW],
                            scalar1=iotac_s[:, 0:1], scalar2=float(kt * 128),
                            op0=ALU.add, op1=ALU.add))
                    else:
                        q0 = qh * QHW
                        sGd.inc(g.tensor_scalar(
                            out=dst, in0=iotab_s[:, q0:q0 + QHW],
                            scalar1=iotac_s[:, 0:1], scalar2=float(-kt * 128),
                            op0=ALU.subtract, op1=ALU.add))
                    mDelta = sGd.n
                    s.wait_ge(sGd.h, mDelta)
                    s.wait_ge(sV.h, war)
                    if tocache:
                        sS.inc(s.activation(out=dst, in_=dst, func=AF.Abs))
                        s.wait_ge(sS.h, sS.n)
                    sS.inc(s.activation(out=dst, in_=dst, func=AF.Ln,
                                        scale=pacols[:, 2 * h + 1:2 * h + 2],
                                        bias=1.0))
                    if tocache:
                        cacheMark[kt] = sS.n
                    return dst, sS.n

                def emit_evac(qh, pvstop):
                    nonlocal mZbLast
                    for seq in range(BC):
                        pA = (qh * BC + seq) % 2
                        v.wait_ge(sT.h, pvstop[seq])
                        sV.inc(v.tensor_copy(aot[:], ppvs[seq][0:64, :]))
                        mAot = sV.n
                        v.wait_ge(sT.h, mZbLast)
                        sV.inc(v.tensor_copy(zsb[:], ppvs[seq][64:65, :]))
                        mZc = sV.n
                        s.wait_ge(sV.h, mZc)
                        sS.inc(s.activation(out=zsb[:], in_=zsb[:], func=AF.Ln))
                        s.wait_ge(sS.h, sS.n)
                        sS.inc(s.activation(out=zsb[:], in_=zsb[:],
                                            func=AF.Exp, scale=-1.0))
                        mZinv = sS.n
                        t.wait_ge(sS.h, mZinv)
                        t.wait_ge(sV.h, mAot)
                        for half in range(2):
                            ins = t.matmul(
                                pbig[0:64, seq * QHW + half * 512:
                                     seq * QHW + (half + 1) * 512],
                                onesf[0:1, 0:64],
                                zsb[0:1, half * 512:(half + 1) * 512],
                                start=True, stop=True)
                        sT.inc(ins)
                        mZbLast = sT.n
                        v.wait_ge(sT.h, mZbLast)
                        v.wait_ge(sAO[pA].h, sAO[pA].n)
                        sV.inc(v.tensor_tensor(out=aobuf[:, pA * QHW:(pA + 1) * QHW],
                                               in0=aot[:],
                                               in1=pbig[0:64, seq * QHW:(seq + 1) * QHW],
                                               op=ALU.mult))
                        lastEvac[seq] = sV.n
                        q.wait_ge(sV.h, lastEvac[seq])
                        sAO[pA].inc(q.dma_start(
                            out=aod[seq, h * 64:(h + 1) * 64, qh * QHW:(qh + 1) * QHW],
                            in_=aobuf[:, pA * QHW:(pA + 1) * QHW]), 16)

                units = [(qh, kt) for qh in range(NQH) for kt in range(KT)]
                kbnext = kb_prep(*units[0])
                pend = []
                pvStop = [0, 0]

                def emit_pv(pqh, pkt, pexp, pp2):
                    t.wait_ge(sS.h, pexp)
                    if pkt == 0:
                        t.wait_ge(sV.h, lastEvac[0])
                        t.wait_ge(sV.h, lastEvac[1])
                        t.wait_ge(sV.h, mMcol)
                    for seq in range(BC):
                        vof = (seq * KT + pkt) * (NH * 65) + h * 65
                        ab = (seq * 2 + pp2) * QHW
                        for half in range(2):
                            ins = t.matmul(
                                ppvs[seq][:, half * 512:(half + 1) * 512],
                                v_sb2[:, vof:vof + 65],
                                attnT[:, ab + half * 512: ab + (half + 1) * 512],
                                start=(pkt == 0), stop=(pkt == KT - 1))
                        sT.inc(ins)
                        lastPV[seq * 2 + pp2] = sT.n
                        if pkt == KT - 1:
                            pvStop[seq] = sT.n
                    if pkt == KT - 1:
                        emit_evac(pqh, pvStop)

                for i, (qh, kt) in enumerate(units):
                    kbl, mLn = kbnext
                    kbnext = kb_prep(*units[i + 1]) if i + 1 < len(units) else None
                    p2 = kt % 2
                    mQKs = [0, 0]
                    t.wait_ge(sQF[sl].h, loadQ)
                    t.wait_ge(sKF[sl].h, loadK)
                    for seq in range(BC):
                        sof = qof + seq * S
                        t.wait_ge(sV.h, lastBias[seq])
                        for half in range(2):
                            ins = t.matmul(
                                psB[:, seq * QHW + half * 512:
                                    seq * QHW + (half + 1) * 512],
                                kfb[0:64, sof + kt * 128: sof + (kt + 1) * 128],
                                qfb[0:64, sof + qh * QHW + half * 512:
                                    sof + qh * QHW + (half + 1) * 512],
                                start=True, stop=True)
                        sT.inc(ins)
                        mQKs[seq] = sT.n
                    mQKlast[sl] = (sT.h, sT.n)
                    if len(pend) == 2:
                        emit_pv(*pend.pop(0))
                    for seq in range(BC):
                        v.wait_ge(sT.h, mQKs[seq])
                        v.wait_ge(sS.h, mLn)
                        v.wait_ge(sS.h, sbtExp[p2])
                        sV.inc(v.scalar_tensor_tensor(
                            out=sbt[:, (p2 * BC + seq) * QHW:(p2 * BC + seq + 1) * QHW],
                            in0=kbl, scalar=pacols[:, 2 * h:2 * h + 1],
                            in1=psB[:, seq * QHW:(seq + 1) * QHW],
                            op0=ALU.mult, op1=ALU.add))
                        lastBias[seq] = sV.n
                    cached = (qh == 1 and kt >= 8)
                    tocache = (qh == 0 and kt < 8)
                    if cached:
                        cacheRead[kt - 8] = sV.n
                    elif tocache:
                        cacheRead[kt] = sV.n
                    else:
                        kbtBbias[kt % 2] = sV.n
                    s.wait_ge(sV.h, lastBias[1])
                    s.wait_ge(sV.h, mMcol)
                    for seq in range(BC):
                        if seq * 2 + p2 in lastPV:
                            s.wait_ge(sT.h, lastPV[seq * 2 + p2])
                    aout = AP(attnT[:].tensor, p2 * QHW,
                              [[2 * BC * QHW, 128], [2 * QHW, BC], [1, QHW]])
                    sS.inc(s.activation(out=aout,
                                        in_=sbt[:, p2 * BC * QHW:(p2 + 1) * BC * QHW],
                                        func=AF.Exp, scale=0.125,
                                        bias=mcol[:, sl:sl + 1]))
                    sbtExp[p2] = sS.n
                    pend.append((qh, kt, sS.n, p2))
                for pe in pend:
                    emit_pv(*pe)
        mEndA2 = dict(t=sT.n, v=sV.n, s=sS.n)

    with ExitStack() as a4st:
        a4sb = lambda n, sh, d: a4st.enter_context(nc.sbuf_tensor(n, sh, d))
        FCA = NH * 64 // 128
        ao_fm = a4sb("ao_fm", [128, 2 * FCA * S], BF16)
        owt_sb = a4sb("owt_sb", [128, FCA * D], BF16)
        x_sb4 = a4sb("x_sb4", [128, 2 * D], F32)
        with nc.psum_tensor("pyo", [128, 2 * 1024], F32) as pyo:
            for eng in (g, q, t, v, s):
                eng.wait_ge(sT.h, mEndA2["t"])
                eng.wait_ge(sV.h, mEndA2["v"])
                eng.wait_ge(sS.h, mEndA2["s"])
            for sm in (sQF[0], sQF[1], sKF[0], sKF[1], sAO[0], sAO[1]):
                g.wait_ge(sm.h, sm.n)
                q.wait_ge(sm.h, sm.n)
            sOW.inc(g.dma_start(
                out=owt_sb[:].rearrange("p (f w) -> p f w", f=FCA),
                in_=out_wT.rearrange("(f p) o -> p f o", p=128)), 16)
            mVadd = [0, 0]
            for seq in range(BC):
                slf = seq % 2
                sAF[slf].inc(q.dma_start(
                    out=ao_fm[:, slf * FCA * S:(slf + 1) * FCA * S]
                        .rearrange("p (f w) -> p f w", f=FCA),
                    in_=aod[seq].rearrange("(f p) w -> p f w", p=128)), 16)
                for tt in range(KT):
                    sl = tt % 2
                    q.wait_ge(sEo[sl].h, sEo[sl].n)
                    q.wait_ge(sV.h, mVadd[sl])
                    sX[sl].inc(q.dma_start(
                        out=x_sb4[:, sl * D:(sl + 1) * D],
                        in_=x_src[seq * S + tt * 128: seq * S + (tt + 1) * 128, :]), 16)
                    pyb = pyo[:, sl * 1024: sl * 1024 + 1024]
                    t.wait_ge(sAF[slf].h, sAF[slf].n)
                    t.wait_ge(sOW.h, sOW.n)
                    t.wait_ge(sV.h, mVadd[sl])
                    for oh in range(2):
                        for j in range(FCA):
                            ins = t.matmul(
                                pyb[:, oh * 512: oh * 512 + 512],
                                ao_fm[:, slf * FCA * S + j * S + tt * 128:
                                      slf * FCA * S + j * S + (tt + 1) * 128],
                                owt_sb[:, j * D + oh * 512: j * D + oh * 512 + 512],
                                start=(j == 0), stop=(j == FCA - 1))
                        sT.inc(ins)
                    mMMa = sT.n
                    v.wait_ge(sT.h, mMMa)
                    v.wait_ge(sX[sl].h, sX[sl].n)
                    for oh in range(2):
                        et = x_sb4[:, sl * D + oh * 512: sl * D + oh * 512 + 512]
                        sV.inc(v.tensor_tensor(out=et, in0=et,
                                               in1=pyb[:, oh * 512: oh * 512 + 512],
                                               op=ALU.add))
                    mVadd[sl] = sV.n
                    q.wait_ge(sV.h, mVadd[sl])
                    sEo[sl].inc(q.dma_start(
                        out=x_dst[seq * S + tt * 128: seq * S + (tt + 1) * 128, :],
                        in_=x_sb4[:, sl * D:(sl + 1) * D]), 16)
    ast.close()




def _host_inputs(cfg, L0_attn_qkv_w, L0_attn_out_w, L0_attn_bias_p, L0_attn_bias_a,
                 L0_attn_rope_freqs, L0_ffn_w_router, L0_ffn_w1, L0_ffn_b1, L0_ffn_w2,
                 L0_ffn_b2):
    import ml_dtypes
    return {
        "qkv_wT": np.ascontiguousarray(np.asarray(L0_attn_qkv_w, np.float32).T),
        "out_wT": np.ascontiguousarray(np.asarray(L0_attn_out_w, np.float32).T),
        "bias_p": np.asarray(L0_attn_bias_p, np.float32).reshape(1, cfg.NH),
        "bias_a": np.asarray(L0_attn_bias_a, np.float32).reshape(1, cfg.NH),
        "rfreqs": np.asarray(L0_attn_rope_freqs, np.float32).reshape(1, cfg.HD // 2),
        "w_router": np.ascontiguousarray(np.asarray(L0_ffn_w_router, np.float32)),
        "w1": np.ascontiguousarray(
            np.asarray(L0_ffn_w1, np.float32)
            .reshape(cfg.E, 8, 128, cfg.HC, 128).transpose(0, 3, 2, 1, 4)
            .reshape(cfg.E, cfg.HC, 128, 8 * 128)),
        "b1l": np.ascontiguousarray(
            np.asarray(L0_ffn_b1, np.float32).reshape(cfg.E, cfg.HC, 128).transpose(2, 0, 1)),
        "w2": np.ascontiguousarray(np.asarray(L0_ffn_w2, np.float32)),
        "b2": np.ascontiguousarray(np.asarray(L0_ffn_b2, np.float32)),
        "iotab": np.tile(np.arange(cfg.S, dtype=np.float32)[None, :], (128, 1)),
        "iota_row": np.arange(max(cfg.S, 128), dtype=np.float32)[None, :],
        "eyeb": np.eye(128, dtype=ml_dtypes.bfloat16),
        "eyef": np.eye(128, dtype=np.float32),
    }


def kernel(L0_attn_x, L0_attn_qkv_w, L0_attn_out_w, L0_attn_bias_p, L0_attn_bias_a,
           L0_attn_rope_freqs, L0_ffn_w_router, L0_ffn_w1, L0_ffn_b1, L0_ffn_w2,
           L0_ffn_b2):
    cfg = Cfg()
    nc = build_program(cfg)
    x = np.ascontiguousarray(np.asarray(L0_attn_x, dtype=np.float32))
    shared = _host_inputs(cfg, L0_attn_qkv_w, L0_attn_out_w, L0_attn_bias_p,
                          L0_attn_bias_a, L0_attn_rope_freqs, L0_ffn_w_router,
                          L0_ffn_w1, L0_ffn_b1, L0_ffn_w2, L0_ffn_b2)
    in_maps = []
    for core in range(8):
        m = dict(shared)
        m["x_in"] = np.ascontiguousarray(x[2 * core:2 * core + 2].reshape(cfg.T, D))
        in_maps.append(m)
    res = run_bass_kernel_spmd(nc, in_maps, core_ids=list(range(8)))
    out = np.stack([r["y_out"].reshape(2, cfg.S, D) for r in res.results])
    return out.reshape(16, cfg.S, D)


# version 08
# speedup vs baseline: 1.1167x; 1.1167x over previous
import math
from contextlib import ExitStack

import numpy as np
import concourse.bass as bass
import concourse.mybir as mybir
from concourse.bass_utils import run_bass_kernel_spmd
from concourse.ap import AP

F32 = mybir.dt.float32
BF16 = mybir.dt.bfloat16
I32 = mybir.dt.int32
AF = mybir.ActivationFunctionType
ALU = mybir.AluOpType
AX = mybir.AxisListType

D = 1024
FC = D // 128


class Cfg:
    def __init__(self, T=4096, E=8, HID=4096, S=2048, NH=16, emit=(1, 2, 3), taps=()):
        self.T = T
        self.E = E
        self.HID = HID
        self.HC = HID // 128
        self.S = S
        self.BC = T // S
        self.NH = NH
        self.HD = 64
        self.QT = min(1024, T)
        self.NQ = T // self.QT
        self.NTQ = self.QT // 128
        self.TB = self.QT // 512
        self.KT = S // 128
        self.QHW = min(1024, S)
        self.NQH = S // self.QHW
        self.emit = emit
        self.taps = taps


class Sem:
    def __init__(self, nc, st, name, prime=0):
        self.h = st.enter_context(nc.semaphore(name))
        self.n = 0
        self.prime = prime
        if prime:
            nc.sync.sem_inc(self.h, prime)
            self.n = prime
    def inc(self, ins, k=1):
        ins.then_inc(self.h, k)
        self.n += k
        return ins


_RC_N = [0]
_NM_N = [0]
def _nm(n):
    _NM_N[0] += 1
    return f"{n}_{_NM_N[0]}"
class RC:
    def __init__(self, eng, name, init):
        self.eng = eng
        _RC_N[0] += 1
        self.r = eng.alloc_register(f"{name}_{_RC_N[0]}")
        eng.reg_mov(self.r, init)
    def use(self, step):
        r = self.r
        return r, step
    def wait(self, sem, step):
        self.eng.wait_ge(sem, self.r)
        if step:
            self.eng.reg_add(self.r, self.r, step)
    def bump(self, k):
        if k:
            self.eng.reg_add(self.r, self.r, k)


def build_program(cfg: Cfg):
    nc = bass.Bass("TRN2", target_bir_lowering=False, debug=False)
    c = cfg
    st = ExitStack()

    dt = nc.dram_tensor
    x_in = dt("x_in", [c.T, D], F32, kind="ExternalInput").ap()
    qkv_wT = dt("qkv_wT", [D, 3 * c.NH * c.HD], F32, kind="ExternalInput").ap()
    out_wT = dt("out_wT", [c.NH * c.HD, D], F32, kind="ExternalInput").ap()
    bias_p = dt("bias_p", [1, c.NH], F32, kind="ExternalInput").ap()
    bias_a = dt("bias_a", [1, c.NH], F32, kind="ExternalInput").ap()
    rfreqs = dt("rfreqs", [1, c.HD // 2], F32, kind="ExternalInput").ap()
    w_router = dt("w_router", [D, c.E], F32, kind="ExternalInput").ap()
    w1 = dt("w1", [c.E, c.HC, 128, FC * 128], F32, kind="ExternalInput").ap()
    b1l = dt("b1l", [128, c.E, c.HC], F32, kind="ExternalInput").ap()
    w2 = dt("w2", [c.E, c.HID, D], F32, kind="ExternalInput").ap()
    b2 = dt("b2", [c.E, D], F32, kind="ExternalInput").ap()
    iotab = dt("iotab", [128, c.S], F32, kind="ExternalInput").ap()
    iota_row = dt("iota_row", [1, max(c.S, 128)], F32, kind="ExternalInput").ap()
    eyeb_in = dt("eyeb", [128, 128], BF16, kind="ExternalInput").ap()
    eyef_in = dt("eyef", [128, 128], F32, kind="ExternalInput").ap()
    y_out = dt("y_out", [c.T, D], F32, kind="ExternalOutput").ap()

    def scratch(name, shape, dtp):
        kind = "ExternalOutput" if name in c.taps else "Internal"
        return dt(name, shape, dtp, kind=kind).ap()
    x1d = scratch("x1", [c.T, D], F32)
    x2d = scratch("x2", [c.T, D], F32)
    aod = scratch("ao", [c.BC, c.NH * c.HD, c.S], BF16)
    qfd = scratch("qf", [c.BC, c.NH, 64, c.S], BF16)
    kfd = scratch("kf", [c.BC, c.NH, 64, c.S], BF16)

    sb = lambda name, shape, dtp: st.enter_context(nc.sbuf_tensor(name, shape, dtp))
    eyeb = sb("eyeb_s", [128, 128], BF16)
    eyef = sb("eyef_s", [128, 128], F32)
    iotac_s = sb("iotac_s", [128, 1], F32)
    b1sb = sb("b1sb", [128, c.E * c.HC], F32)
    b2sb = sb("b2sb", [c.E, D], BF16)
    wrt_sb = sb("wrt_sb", [128, FC * c.E], BF16)
    pa_sb = sb("pa_sb", [1, 2 * c.NH], F32)
    freq_sb = sb("freq_sb", [1, c.HD // 2], F32)
    NT_S = c.S // 128
    cs_sb = sb("cs_sb", [128, NT_S * c.HD], F32)
    junk = sb("junk", [128, 512], F32)
    onesb = sb("onesb", [1, 128], BF16)
    epscol = sb("epscol", [128, 1], F32)

    S_ = lambda name, prime=0: Sem(nc, st, name, prime)
    sPre = S_("sPre")
    sPreG = S_("sPreG")
    sPos = S_("sPos")
    sV = S_("sV")
    sS = S_("sS")
    sT = S_("sT")
    sX = [S_("sX0"), S_("sX1")]
    sX2 = [S_("sX2a"), S_("sX2b")]
    NB = 4
    sW1 = [S_(f"sW1_{i}") for i in range(NB)]
    sW2 = S_("sW2")
    sB1 = [S_("sB1a"), S_("sB1b")]
    sWc = [S_("sWca"), S_("sWcb")]
    sMM = S_("sMM", prime=64)
    sG = S_("sG", prime=64)
    sF2 = S_("sF2", prime=64)
    sAcc = S_("sAcc", prime=64)
    sOut = S_("sOut")
    sEo = [S_("sEoA"), S_("sEoB")]

    q = nc.sync
    g = nc.gpsimd
    t = nc.tensor
    v = nc.vector
    s = nc.scalar
    dma_sems = [sPre, sPreG, sPos, sX[0], sX[1], sX2[0], sX2[1], sW1[0], sW1[1],
                sW1[2], sW1[3], sW2, sB1[0], sB1[1], sWc[0], sWc[1], sEo[0], sEo[1],
                sOut]

    def sublayer_entry():
        for eng in (q, g, t, v, s):
            for sm in dma_sems:
                eng.wait_ge(sm.h, sm.n)
            eng.wait_ge(sV.h, sV.n)
            eng.wait_ge(sS.h, sS.n)
            eng.wait_ge(sT.h, sT.n)

    sPre.inc(q.dma_start(out=eyeb[:], in_=eyeb_in), 16)
    sPre.inc(q.dma_start(out=eyef[:], in_=eyef_in), 16)
    sPre.inc(q.dma_start(out=iotac_s[:],
                         in_=AP(iota_row.tensor, 0, [[1, 128], [1, 1]])), 16)
    sPre.inc(q.dma_start(out=b1sb[:].rearrange("p (e h) -> p e h", e=c.E), in_=b1l), 16)
    sPre.inc(q.dma_start(out=pa_sb[:, 0:c.NH], in_=bias_p), 16)
    sPre.inc(q.dma_start(out=pa_sb[:, c.NH:2 * c.NH], in_=bias_a), 16)
    sPre.inc(q.dma_start(out=freq_sb[:], in_=rfreqs), 16)
    sPreG.inc(g.dma_start(out=b2sb[:], in_=b2), 16)
    sPreG.inc(g.dma_start(out=wrt_sb[:].rearrange("p (f e) -> p f e", f=FC),
                          in_=w_router.rearrange("(f p) e -> p f e", p=128)), 16)

    v.wait_ge(sPre.h, sPre.n)
    v.wait_ge(sPreG.h, sPreG.n)
    sV.inc(v.tensor_scalar(out=pa_sb[:], in0=pa_sb[:], scalar1=0.01, scalar2=None,
                           op0=ALU.max))
    sV.inc(v.memset(onesb[:], 1.0))
    sV.inc(v.memset(epscol[:], 1e-5))

    HD2 = c.HD // 2
    with nc.psum_tensor("pcs", [128, 512], F32) as pcs, \
         nc.sbuf_tensor("posrow", [1, max(c.S, 128)], F32) as posrow, \
         nc.sbuf_tensor("angf", [128, c.HD], F32) as angf, \
         nc.sbuf_tensor("angi", [128, c.HD], I32) as angi, \
         nc.sbuf_tensor("angr", [128, c.HD], F32) as angr:
        sPos.inc(q.dma_start(out=posrow[:], in_=iota_row), 16)
        for tt in range(NT_S):
            t.wait_ge(sPre.h, sPre.n)
            t.wait_ge(sPos.h, sPos.n)
            t.wait_ge(sV.h, sV.n)
            sT.inc(t.matmul(pcs[:, 0:HD2], posrow[:, tt * 128:(tt + 1) * 128],
                            freq_sb[:], start=True, stop=True))
            v.wait_ge(sT.h, sT.n)
            v.wait_ge(sS.h, sS.n)
            sV.inc(v.tensor_scalar(out=angf[:, 0:HD2], in0=pcs[:, 0:HD2],
                                   scalar1=float(np.pi / 2), scalar2=None, op0=ALU.add))
            sV.inc(v.tensor_copy(angf[:, HD2:c.HD], pcs[:, 0:HD2]))
            v.wait_ge(sV.h, sV.n)
            sV.inc(v.tensor_scalar(out=angr[:], in0=angf[:],
                                   scalar1=float(1.0 / (2 * np.pi)), scalar2=None,
                                   op0=ALU.mult))
            v.wait_ge(sV.h, sV.n)
            sV.inc(v.tensor_copy(angi[:], angr[:]))
            v.wait_ge(sV.h, sV.n)
            sV.inc(v.tensor_copy(angr[:], angi[:]))
            v.wait_ge(sV.h, sV.n)
            sV.inc(v.scalar_tensor_tensor(out=angf[:], in0=angr[:],
                                          scalar=float(-2 * np.pi), in1=angf[:],
                                          op0=ALU.mult, op1=ALU.add))
            v.wait_ge(sV.h, sV.n)
            sV.inc(v.tensor_scalar(out=angr[:], in0=angf[:], scalar1=float(np.pi),
                                   scalar2=None, op0=ALU.is_gt))
            v.wait_ge(sV.h, sV.n)
            sV.inc(v.scalar_tensor_tensor(out=angf[:], in0=angr[:],
                                          scalar=float(-2 * np.pi), in1=angf[:],
                                          op0=ALU.mult, op1=ALU.add))
            s.wait_ge(sV.h, sV.n)
            sS.inc(s.activation(out=cs_sb[:, tt * c.HD:(tt + 1) * c.HD], in_=angf[:],
                                func=AF.Sin))

    def ffn_sublayer(x_src, x_dst):
        E, HC, QT, NTQ, TB = c.E, c.HC, c.QT, c.NTQ, c.TB
        assert (E * HC) % NB == 0
        fst = ExitStack()
        fsb = lambda n, sh, d: fst.enter_context(nc.sbuf_tensor(_nm(n), sh, d))
        x_sb = fsb("x_sb", [128, 2 * D], F32)
        xhat_tm = fsb("xhat_tm", [128, D], BF16)
        xhat_fm = fsb("xhat_fm", [128, FC * QT], BF16)
        red = fsb("red", [128, 8], F32)
        w1buf = fsb("w1buf", [128, NB * D], BF16)
        w2_sb = fsb("w2_sb", [128, HC * D], BF16)
        g_sb = fsb("g_sb", [128, HC * QT], BF16)
        acc = fsb("acc", [128, NTQ * D], F32)
        wq_sb = fsb("wq_sb", [128, NTQ * E], F32)
        wtokT = fsb("wtokT", [c.E, QT], BF16)
        pexp = fsb("pexp", [128, E], F32)
        probs = fsb("probs", [128, E], F32)
        msk = fsb("msk", [128, E], F32)
        p2t = fsb("p2t", [128, E], F32)
        b1cur = fsb("b1cur", [128, 2 * HC], F32)
        wcur = fsb("wcur", [128, 2 * NTQ], F32)
        wq2 = fsb("wq2", [128, E * NTQ], F32)
        pst = ExitStack()
        ph0 = pst.enter_context(nc.psum_tensor(_nm("ph0"), [128, 512], F32))
        ph1 = pst.enter_context(nc.psum_tensor(_nm("ph1"), [128, 512], F32))
        po = pst.enter_context(nc.psum_tensor(_nm("po"), [128, 4 * 512], F32))
        pr = pst.enter_context(nc.psum_tensor(_nm("pr"), [128, 512], F32))
        pt = pst.enter_context(nc.psum_tensor(_nm("pt"), [128, 1024], BF16))
        phs = [ph0, ph1]

        sublayer_entry()
        for qq in range(c.NQ):
            tok0 = qq * QT
            v.wait_ge(sMM.h, sMM.n)
            for tt in range(NTQ):
                sl = tt % 2
                r0 = (tt % 2) * 4
                q.wait_ge(sV.h, sV.n)
                q.wait_ge(sS.h, sS.n)
                sX[sl].inc(q.dma_start(
                    out=x_sb[:, sl * D:(sl + 1) * D],
                    in_=x_src[tok0 + tt * 128: tok0 + (tt + 1) * 128, :]), 16)
                v.wait_ge(sX[sl].h, sX[sl].n)
                v.wait_ge(sT.h, sT.n)
                xs = x_sb[:, sl * D:(sl + 1) * D]
                sV.inc(v.reduce_sum(out=red[:, r0:r0 + 1], in_=xs, axis=AX.X, negate=True))
                v.wait_ge(sV.h, sV.n)
                sV.inc(v.tensor_scalar(out=red[:, r0 + 1:r0 + 2], in0=red[:, r0:r0 + 1],
                                       scalar1=float(1.0 / D), scalar2=None, op0=ALU.mult))
                v.wait_ge(sV.h, sV.n)
                sV.inc(v.tensor_scalar(out=xs, in0=xs, scalar1=red[:, r0 + 1:r0 + 2],
                                       scalar2=None, op0=ALU.add))
                s.wait_ge(sV.h, sV.n)
                sS.inc(s.activation(out=junk[:], in_=xs[:, 0:512], func=AF.Square,
                                    accum_out=red[:, r0 + 2:r0 + 3]))
                s.wait_ge(sS.h, sS.n)
                sS.inc(s.activation(out=junk[:], in_=xs[:, 512:1024], func=AF.Square,
                                    accum_out=red[:, r0 + 3:r0 + 4]))
                v.wait_ge(sS.h, sS.n)
                sV.inc(v.tensor_tensor(out=red[:, r0 + 2:r0 + 3], in0=red[:, r0 + 2:r0 + 3],
                                       in1=red[:, r0 + 3:r0 + 4], op=ALU.add))
                v.wait_ge(sV.h, sV.n)
                sV.inc(v.tensor_scalar(out=red[:, r0 + 3:r0 + 4], in0=red[:, r0 + 2:r0 + 3],
                                       scalar1=float(1.0 / D), scalar2=1e-5,
                                       op0=ALU.mult, op1=ALU.add))
                s.wait_ge(sV.h, sV.n)
                sS.inc(s.activation(out=red[:, r0 + 2:r0 + 3], in_=red[:, r0 + 3:r0 + 4],
                                    func=AF.Ln))
                s.wait_ge(sS.h, sS.n)
                sS.inc(s.activation(out=red[:, r0 + 3:r0 + 4], in_=red[:, r0 + 2:r0 + 3],
                                    func=AF.Exp, scale=-0.5))
                v.wait_ge(sS.h, sS.n)
                sV.inc(v.tensor_scalar(out=xhat_tm[:, 0:D], in0=xs,
                                       scalar1=red[:, r0 + 3:r0 + 4], scalar2=None,
                                       op0=ALU.mult))
                t.wait_ge(sV.h, sV.n)
                for j in range(FC):
                    ins = t.transpose(pt[:, j * 128:(j + 1) * 128],
                                      xhat_tm[:, j * 128:(j + 1) * 128],
                                      eyeb[:])
                sT.inc(ins)
                v.wait_ge(sT.h, sT.n)
                sV.inc(v.tensor_copy(
                    xhat_fm[:].rearrange("p (f w) -> p f w", f=FC)[:, :, tt * 128:(tt + 1) * 128],
                    pt[:].rearrange("p (f w) -> p f w", f=FC)))
                t.wait_ge(sV.h, sV.n)
                for j in range(FC):
                    ins = t.matmul(pr[:, 0:E],
                                   xhat_fm[:, j * QT + tt * 128: j * QT + (tt + 1) * 128],
                                   wrt_sb[:, j * E:(j + 1) * E],
                                   start=(j == 0), stop=(j == FC - 1))
                sT.inc(ins)
                v.wait_ge(sT.h, sT.n)
                sV.inc(v.reduce_max(out=red[:, r0 + 3:r0 + 4], in_=pr[:, 0:E], axis=AX.X,
                                    negate=True))
                s.wait_ge(sV.h, sV.n)
                sS.inc(s.activation(out=pexp[:], in_=pr[:, 0:E], func=AF.Exp,
                                    bias=red[:, r0 + 3:r0 + 4], accum_out=red[:, r0:r0 + 1]))
                v.wait_ge(sS.h, sS.n)
                sV.inc(v.reciprocal(out=red[:, r0 + 1:r0 + 2], in_=red[:, r0:r0 + 1]))
                v.wait_ge(sV.h, sV.n)
                sV.inc(v.tensor_scalar(out=probs[:], in0=pexp[:],
                                       scalar1=red[:, r0 + 1:r0 + 2], scalar2=None,
                                       op0=ALU.mult))
                v.wait_ge(sV.h, sV.n)
                sV.inc(v.reduce_max(out=red[:, r0:r0 + 1], in_=probs[:], axis=AX.X))
                v.wait_ge(sV.h, sV.n)
                sV.inc(v.tensor_scalar(out=msk[:], in0=probs[:], scalar1=red[:, r0:r0 + 1],
                                       scalar2=None, op0=ALU.is_ge))
                v.wait_ge(sV.h, sV.n)
                sV.inc(v.scalar_tensor_tensor(out=p2t[:], in0=msk[:], scalar=-1e30,
                                              in1=probs[:], op0=ALU.mult, op1=ALU.add))
                v.wait_ge(sV.h, sV.n)
                sV.inc(v.reduce_max(out=red[:, r0 + 1:r0 + 2], in_=p2t[:], axis=AX.X))
                v.wait_ge(sV.h, sV.n)
                sV.inc(v.tensor_scalar(out=msk[:], in0=probs[:],
                                       scalar1=red[:, r0 + 1:r0 + 2], scalar2=None,
                                       op0=ALU.is_ge))
                v.wait_ge(sV.h, sV.n)
                sV.inc(v.scalar_tensor_tensor(out=wq_sb[:, tt * E:(tt + 1) * E],
                                              in0=probs[:], scalar=0.5, in1=msk[:],
                                              op0=ALU.mult, op1=ALU.mult))
                v.wait_ge(sV.h, sV.n)
                sV.inc(v.tensor_copy(
                    AP(wq2[:].tensor, tt, [[E * NTQ, 128], [NTQ, E]]),
                    wq_sb[:, tt * E:(tt + 1) * E]))
                t.wait_ge(sV.h, sV.n)
                sT.inc(t.transpose(pr[0:E, 128:256], wq_sb[:, tt * E:(tt + 1) * E], eyef[:]))
                v.wait_ge(sT.h, sT.n)
                sV.inc(v.tensor_copy(wtokT[:, tt * 128:(tt + 1) * 128], pr[0:E, 128:256]))
            v.wait_ge(sV.h, sV.n)
            sV.inc(v.memset(acc[:], 0.0))

            f2base = sF2.n
            gbase = sG.n
            mmbase = sMM.n
            accbase = sAcc.n
            w1base = [sW1[j].n for j in range(NB)]
            w2base = sW2.n
            for e in range(E):
                sl_e = e % 2
                g.wait_ge(sG.h, gbase + max(0, (e - 1)) * (HC * TB) if e >= 2 else 0)
                sB1[sl_e].inc(g.dma_start(out=b1cur[:, sl_e * HC:(sl_e + 1) * HC],
                                          in_=b1l[:, e, :]), 16)
                g.wait_ge(sV.h, sV.n)
                g.wait_ge(sAcc.h, accbase + max(0, (e - 1)) * (NTQ * 2) if e >= 2 else 0)
                sWc[sl_e].inc(g.dma_start(out=wcur[:, sl_e * NTQ:(sl_e + 1) * NTQ],
                                          in_=wq2[:, e * NTQ:(e + 1) * NTQ]), 16)
                for hc in range(HC):
                    ci = e * HC + hc
                    sl = ci % NB
                    g.wait_ge(sG.h, gbase + (ci - NB + 1) * TB)
                    sW1[sl].inc(g.dma_start(
                        out=w1buf[:, sl * D:(sl + 1) * D],
                        in_=w1[e, hc]), 16)
                for ohh in range(2):
                    prev = (e - 1) * NTQ * 2 + (ohh + 1) * NTQ if e >= 1 else 0
                    g.wait_ge(sF2.h, f2base + prev)
                    sW2.inc(g.dma_start(
                        out=w2_sb[:, ohh * HC * 512:(ohh + 1) * HC * 512]
                            .rearrange("p (h w) -> p h w", h=HC),
                        in_=w2[e].rearrange("(h p) d -> p h d", p=128)
                            [:, :, ohh * 512:(ohh + 1) * 512]), 16)

            rW1 = [RC(t, f"rW1_{j}", w1base[j] + 16) for j in range(NB)]
            rG = RC(t, "rG", gbase - 1)
            rGE = RC(t, "rGE", gbase + HC * TB)
            rW2 = RC(t, "rW2", w2base + 16)
            rAc = RC(t, "rAc", accbase - 3)
            with t.Fori(0, E) as ei:
                for gi in range(HC // NB):
                    for j in range(NB):
                        rW1[j].wait(sW1[j].h, 16)
                        for tb in range(TB):
                            k = (gi * NB + j) * TB + tb
                            rG.wait(sG.h, 1)
                            ps = phs[k % 2]
                            for fc_ in range(FC):
                                ins = t.matmul(
                                    ps[:],
                                    w1buf[:, j * D + fc_ * 128: j * D + (fc_ + 1) * 128],
                                    xhat_fm[:, fc_ * QT + tb * 512: fc_ * QT + tb * 512 + 512],
                                    start=(fc_ == 0), stop=(fc_ == FC - 1))
                            ins.then_inc(sMM.h, 1)
                rGE.wait(sG.h, HC * TB)
                for oh in range(2):
                    rW2.wait(sW2.h, 16)
                    for ts in range(NTQ):
                        g2 = oh * NTQ + ts
                        rAc.wait(sAcc.h, 1)
                        pob = po[:, (g2 % 4) * 512:(g2 % 4) * 512 + 512]
                        for hc in range(HC):
                            ins = t.matmul(
                                pob,
                                g_sb[:, hc * QT + ts * 128: hc * QT + (ts + 1) * 128],
                                w2_sb[:, oh * HC * 512 + hc * 512:
                                      oh * HC * 512 + (hc + 1) * 512],
                                start=(hc == 0), stop=(hc == HC - 1))
                        ins.then_inc(sF2.h, 1)
            for j in range(NB):
                t.free_register(rW1[j].r)
            for rc in (rG, rGE, rW2, rAc):
                t.free_register(rc.r)
            sMM.n = mmbase + E * HC * TB
            sF2.n = f2base + E * NTQ * 2

            rF2s = RC(s, "rF2s", f2base)
            rMMs = RC(s, "rMMs", mmbase + 1)
            rB1w = [RC(s, "rB1w0", sB1[0].n - (E // 2 - 1) * 16),
                    RC(s, "rB1w1", sB1[1].n - (E // 2 - 1) * 16)]
            with s.Fori(0, E // 2) as ei:
                for sl_e in range(2):
                    rF2s.wait(sF2.h, NTQ * 2)
                    rB1w[sl_e].wait(sB1[sl_e].h, 16)
                    for gi in range(HC // NB):
                        for j in range(NB):
                            hcq = gi * NB + j
                            for tb in range(TB):
                                k = hcq * TB + tb
                                rMMs.wait(sMM.h, 1)
                                ps = phs[k % 2]
                                dst = AP(g_sb[:].tensor, hcq * QT + tb * 512,
                                         [[HC * QT, 128], [1, 512]])
                                s.activation(out=dst, in_=ps[:], func=AF.Gelu_apprx_tanh,
                                             bias=b1cur[:, sl_e * HC + hcq: sl_e * HC + hcq + 1]
                                             ).then_inc(sG.h, 1)
            for rc in (rF2s, rMMs, rB1w[0], rB1w[1]):
                s.free_register(rc.r)
            sG.n = gbase + E * HC * TB

            v.wait_ge(sV.h, sV.n)
            rF2v = RC(v, "rF2v", f2base + 1)
            rAcv = RC(v, "rAcv", accbase - (NTQ * 2) + 1)
            rWcw = [RC(v, "rWcw0", sWc[0].n - (E // 2 - 1) * 16),
                    RC(v, "rWcw1", sWc[1].n - (E // 2 - 1) * 16)]
            with v.Fori(0, E // 2) as ei:
                for sl_e in range(2):
                    rWcw[sl_e].wait(sWc[sl_e].h, 16)
                    for oh in range(2):
                        for ts in range(NTQ):
                            rF2v.wait(sF2.h, 1)
                            rAcv.wait(sAcc.h, 1)
                            g2 = oh * NTQ + ts
                            pob = po[:, (g2 % 4) * 512:(g2 % 4) * 512 + 512]
                            at = acc[:, ts * D + oh * 512: ts * D + oh * 512 + 512]
                            wptr = wcur[:, sl_e * NTQ + ts: sl_e * NTQ + ts + 1]
                            v.scalar_tensor_tensor(out=at, in0=pob, scalar=wptr, in1=at,
                                                   op0=ALU.mult, op1=ALU.add).then_inc(sAcc.h, 1)
            for rc in (rF2v, rAcv, rWcw[0], rWcw[1]):
                v.free_register(rc.r)
            sAcc.n = accbase + E * NTQ * 2

            eobase = [sEo[0].n, sEo[1].n]
            for ts in range(NTQ):
                sl = ts % 2
                q.wait_ge(sV.h, sV.n)
                q.wait_ge(sS.h, sS.n)
                if ts >= 2:
                    q.wait_ge(sEo[sl].h, eobase[sl] + (ts // 2) * 16)
                sX2[sl].inc(q.dma_start(
                    out=x_sb[:, sl * D:(sl + 1) * D],
                    in_=x_src[tok0 + ts * 128: tok0 + (ts + 1) * 128, :]), 16)
                t.wait_ge(sPreG.h, sPreG.n)
                t.wait_ge(sAcc.h, sAcc.n)
                for oh in range(2):
                    t.wait_ge(sV.h, sV.n)
                    sT.inc(t.matmul(pr[:, 0:512], wtokT[:, ts * 128:(ts + 1) * 128],
                                    b2sb[:, oh * 512:(oh + 1) * 512], start=True, stop=True))
                    v.wait_ge(sT.h, sT.n)
                    v.wait_ge(sX2[sl].h, sX2[sl].n)
                    et = x_sb[:, sl * D + oh * 512: sl * D + oh * 512 + 512]
                    sV.inc(v.tensor_tensor(out=et, in0=et, in1=pr[:, 0:512], op=ALU.add))
                    v.wait_ge(sV.h, sV.n)
                    at = acc[:, ts * D + oh * 512: ts * D + oh * 512 + 512]
                    sV.inc(v.tensor_tensor(out=et, in0=et, in1=at, op=ALU.add))
                q.wait_ge(sV.h, sV.n)
                sEo[sl].inc(q.dma_start(out=x_dst[tok0 + ts * 128: tok0 + (ts + 1) * 128, :],
                                        in_=x_sb[:, sl * D:(sl + 1) * D]), 16)
        pst.close()
        fst.close()

    def attn_sublayer(x_src, x_dst):
        _emit_attention(nc, c, x_src, x_dst, dict(
            qkv_wT=qkv_wT, out_wT=out_wT, aod=aod, qfd=qfd, kfd=kfd, iotab=iotab,
            eyeb=eyeb, eyef=eyef, iotac_s=iotac_s, pa_sb=pa_sb, epscol=epscol,
            cs_sb=cs_sb, onesb=onesb, junk=junk, sPos=sPos,
            dma_sems=dma_sems, sublayer_entry=sublayer_entry,
            sems=dict(sPre=sPre, sV=sV, sS=sS, sT=sT, sX=sX, sX2=sX2, sOut=sOut,
                      sEo=sEo),
            S_=S_))

    if 1 in c.emit:
        dst1 = x1d if (2 in c.emit or 3 in c.emit or "x1" in c.taps) else y_out
        ffn_sublayer(x_in, dst1)
    if 2 in c.emit:
        src2 = x1d if 1 in c.emit else x_in
        dst2 = x2d if (3 in c.emit or "x2" in c.taps) else y_out
        attn_sublayer(src2, dst2)
    if 3 in c.emit:
        src3 = x2d if 2 in c.emit else (x1d if 1 in c.emit else x_in)
        ffn_sublayer(src3, y_out)

    q.wait_ge(sOut.h, sOut.n)
    q.wait_ge(sEo[0].h, sEo[0].n)
    q.wait_ge(sEo[1].h, sEo[1].n)
    st.close()
    return nc


def _emit_attention(nc, c, x_src, x_dst, env):
    _attn_emit(nc, c, x_src, x_dst, env)


def _attn_emit(nc, c, x_src, x_dst, env):
    qkv_wT, out_wT = env["qkv_wT"], env["out_wT"]
    aod, qfd, kfd, iotab = env["aod"], env["qfd"], env["kfd"], env["iotab"]
    eyeb, eyef, iotac_s, pa_sb = env["eyeb"], env["eyef"], env["iotac_s"], env["pa_sb"]
    cs_sb, onesb, junk = env["cs_sb"], env["onesb"], env["junk"]
    epscol = env["epscol"]
    sems = env["sems"]
    sPre, sV, sS, sT = sems["sPre"], sems["sV"], sems["sS"], sems["sT"]
    sX, sEo = sems["sX"], sems["sEo"]
    S_ = env["S_"]

    q = nc.sync
    g = nc.gpsimd
    t = nc.tensor
    v = nc.vector
    s = nc.scalar

    S, NH, KT, BC = c.S, c.NH, c.KT, c.BC
    QHW, NQH = c.QHW, c.NQH
    NO = 3 * NH * 64 // 512
    NB8 = NH // 8

    sQV = S_("sQV")
    sSp = S_("sSp")
    sQF = [S_("sQF0"), S_("sQF1")]
    sKF = [S_("sKF0"), S_("sKF1")]
    sAO = [S_("sAOa"), S_("sAOb")]
    sAF = [S_("sAFa"), S_("sAFb")]
    sOW = S_("sOW")
    sGd = S_("sGd")
    env["dma_sems"].extend([sQV, sSp, sQF[0], sQF[1], sKF[0], sKF[1],
                            sAO[0], sAO[1], sAF[0], sAF[1], sOW])

    ast = ExitStack()
    asb = lambda n, sh, d: ast.enter_context(nc.sbuf_tensor(n, sh, d))
    iotab_s = asb("iotab_s", [128, S], F32)
    iotabn_s = asb("iotabn_s", [128, S], F32)
    v_sb2 = asb("v_sb2", [128, BC * KT * (NH * 65)], BF16)
    pacols = asb("pacols", [128, 2 * NH], F32)
    onesf = asb("onesf", [1, 128], F32)
    env["sublayer_entry"]()
    sPre.inc(q.dma_start(out=iotab_s[:], in_=iotab), 16)

    v.wait_ge(sPre.h, sPre.n)
    sV.inc(v.memset(onesf[:], 1.0))
    v.wait_ge(sV.h, sV.n)
    sV.inc(v.tensor_scalar(out=iotabn_s[:], in0=iotab_s[:], scalar1=-1.0,
                           scalar2=None, op0=ALU.mult))

    with nc.psum_tensor("ppa", [128, 512], F32) as ppa:
        for h in range(NH):
            t.wait_ge(sV.h, sV.n)
            t.wait_ge(sPre.h, sPre.n)
            sT.inc(t.matmul(ppa[:, 0:1], onesf[:], pa_sb[0:1, h:h + 1],
                            start=True, stop=True))
            sT.inc(t.matmul(ppa[:, 1:2], onesf[:], pa_sb[0:1, NH + h:NH + h + 1],
                            start=True, stop=True))
            v.wait_ge(sT.h, sT.n)
            sV.inc(v.tensor_scalar(out=pacols[:, 2 * h:2 * h + 1], in0=ppa[:, 0:1],
                                   scalar1=-8.0, scalar2=None, op0=ALU.mult))
            sV.inc(v.tensor_copy(pacols[:, 2 * h + 1:2 * h + 2], ppa[:, 1:2]))

    mEndA1 = {}
    with ExitStack() as a1st:
        a1sb = lambda n, sh, d: a1st.enter_context(nc.sbuf_tensor(n, sh, d))
        x_sb = a1sb("x_sb_a", [128, 2 * D], F32)
        junka = a1sb("junka", [128, D], F32)
        xhat_tm = a1sb("xhat_tm_a", [128, D], BF16)
        xt_fm = a1sb("xt_fm", [128, 2 * D], BF16)
        red = a1sb("red_a", [128, 12], F32)
        qkvw_sb = a1sb("qkvw_sb", [128, NO * FC * 512], BF16)
        csrep = a1sb("csrep", [128, KT * 512], F32)
        ropeA = a1sb("ropeA", [128, 256], F32)
        ropeB = a1sb("ropeB", [128, 256], F32)
        qk_tm = a1sb("qk_tm", [128, 512], BF16)
        qstage = a1sb("qstage", [128, 2048], BF16)
        kstage = a1sb("kstage", [128, 2048], BF16)
        with nc.psum_tensor("pq", [128, 2 * 512], F32) as pq, \
             nc.psum_tensor("pt2", [128, 1024], BF16) as pt2:

            g.wait_ge(sPre.h, sPre.n)
            for ob in range(NO):
                sQV.inc(g.dma_start(
                    out=qkvw_sb[:, ob * FC * 512:(ob + 1) * FC * 512]
                        .rearrange("p (f w) -> p f w", f=FC),
                    in_=qkv_wT.rearrange("(f p) o -> p f o", p=128)
                        [:, :, ob * 512:(ob + 1) * 512]), 16)
            v.wait_ge(sS.h, sS.n)
            v.wait_ge(sV.h, sV.n)
            for tt in range(KT):
                for rr in range(8):
                    sV.inc(v.tensor_copy(csrep[:, tt * 512 + rr * 32: tt * 512 + (rr + 1) * 32],
                                         cs_sb[:, tt * 64: tt * 64 + 32]))
                    sV.inc(v.tensor_copy(
                        csrep[:, tt * 512 + 256 + rr * 32: tt * 512 + 256 + (rr + 1) * 32],
                        cs_sb[:, tt * 64 + 32: tt * 64 + 64]))
            mCsrep = sV.n

            mLNx = [0, 0]
            mA0tr = 0
            mXhat = [0, 0]
            mXuse = [0, 0]
            mPQslot = {0: (None, 0), 1: (None, 0)}
            mRope = 0
            mQKtr = 0
            mStage = 0
            mSpillTot = 0
            for seq in range(BC):
                base_tok = seq * S
                for tt in range(KT):
                    sl = tt % 2
                    r0 = 6 * sl
                    q.wait_ge(sV.h, mLNx[sl])
                    sX[sl].inc(q.dma_start(
                        out=x_sb[:, sl * D:(sl + 1) * D],
                        in_=x_src[base_tok + tt * 128: base_tok + (tt + 1) * 128, :]), 16)
                    xs = x_sb[:, sl * D:(sl + 1) * D]
                    s.wait_ge(sX[sl].h, sX[sl].n)
                    sS.inc(s.activation(out=junka[:], in_=xs, func=AF.Copy,
                                        accum_out=red[:, r0:r0 + 1]))
                    s.wait_ge(sS.h, sS.n)
                    sS.inc(s.activation(out=red[:, r0 + 1:r0 + 2], in_=red[:, r0:r0 + 1],
                                        func=AF.Copy, scale=float(-1.0 / D)))
                    s.wait_ge(sS.h, sS.n)
                    sS.inc(s.activation(out=junka[:], in_=xs, func=AF.Square,
                                        bias=red[:, r0 + 1:r0 + 2],
                                        accum_out=red[:, r0 + 2:r0 + 3]))
                    s.wait_ge(sS.h, sS.n)
                    sS.inc(s.activation(out=red[:, r0 + 3:r0 + 4], in_=red[:, r0 + 2:r0 + 3],
                                        func=AF.Ln, scale=float(1.0 / D),
                                        bias=epscol[:, 0:1]))
                    s.wait_ge(sS.h, sS.n)
                    sS.inc(s.activation(out=red[:, r0 + 4:r0 + 5], in_=red[:, r0 + 3:r0 + 4],
                                        func=AF.Exp, scale=-0.5))
                    v.wait_ge(sS.h, sS.n)
                    v.wait_ge(sT.h, mA0tr)
                    sV.inc(v.tensor_scalar(out=xhat_tm[:], in0=xs,
                                           scalar1=red[:, r0 + 1:r0 + 2],
                                           scalar2=red[:, r0 + 4:r0 + 5],
                                           op0=ALU.add, op1=ALU.mult))
                    mLNx[sl] = sV.n
                    t.wait_ge(sV.h, mLNx[sl])
                    t.wait_ge(sV.h, mStage)
                    for j in range(FC):
                        ins = t.transpose(pt2[:, j * 128:(j + 1) * 128],
                                          xhat_tm[:, j * 128:(j + 1) * 128], eyeb[:])
                    sT.inc(ins)
                    mA0tr = sT.n
                    v.wait_ge(sT.h, mA0tr)
                    v.wait_ge(sT.h, mXuse[sl])
                    sV.inc(v.tensor_copy(xt_fm[:, sl * D:(sl + 1) * D], pt2[:]))
                    mXhat[sl] = sV.n
                    for ob in range(NO):
                        pqs = pq[:, (ob % 2) * 512:(ob % 2) * 512 + 512]
                        t.wait_ge(sQV.h, sQV.n)
                        t.wait_ge(sV.h, mXhat[sl])
                        sem_pq, val_pq = mPQslot[ob % 2]
                        if sem_pq is not None:
                            t.wait_ge(sem_pq, val_pq)
                        for j in range(FC):
                            ins = t.matmul(
                                pqs,
                                xt_fm[:, sl * D + j * 128: sl * D + (j + 1) * 128],
                                qkvw_sb[:, ob * FC * 512 + j * 512:
                                        ob * FC * 512 + (j + 1) * 512],
                                start=(j == 0), stop=(j == FC - 1))
                        sT.inc(ins)
                        mMM = sT.n
                        if ob == NO - 1:
                            mXuse[sl] = sT.n
                        if ob < 2 * NB8:
                            is_q = ob < NB8
                            blk = ob if is_q else ob - NB8
                            pv = pqs.rearrange("p (h two d) -> p h two d", two=2, d=32)
                            t1v, t2v = pv[:, :, 0, :], pv[:, :, 1, :]
                            cr = csrep[:, tt * 512: tt * 512 + 256] \
                                .rearrange("p (h d) -> p h d", d=32)
                            sr = csrep[:, tt * 512 + 256: tt * 512 + 512] \
                                .rearrange("p (h d) -> p h d", d=32)
                            qv = qk_tm[:].rearrange("p (h two d) -> p h two d", two=2, d=32)
                            rA = ropeA[:].rearrange("p (h d) -> p h d", d=32)
                            rB = ropeB[:].rearrange("p (h d) -> p h d", d=32)
                            v.wait_ge(sT.h, mMM)
                            v.wait_ge(sT.h, mQKtr)
                            v.wait_ge(sV.h, mCsrep)
                            sV.inc(v.tensor_tensor(out=rA, in0=t1v, in1=cr, op=ALU.mult))
                            sV.inc(v.tensor_tensor(out=rB, in0=t2v, in1=sr, op=ALU.mult))
                            v.wait_ge(sV.h, sV.n)
                            sV.inc(v.tensor_tensor(out=qv[:, :, 0, :], in0=rA, in1=rB,
                                                   op=ALU.subtract))
                            v.wait_ge(sV.h, sV.n)
                            sV.inc(v.tensor_tensor(out=rA, in0=t1v, in1=sr, op=ALU.mult))
                            sV.inc(v.tensor_tensor(out=rB, in0=t2v, in1=cr, op=ALU.mult))
                            v.wait_ge(sV.h, sV.n)
                            sV.inc(v.tensor_tensor(out=qv[:, :, 1, :], in0=rA, in1=rB,
                                                   op=ALU.add))
                            mRope = sV.n
                            mPQslot[ob % 2] = (sV.h, sV.n)
                            stage = qstage if is_q else kstage
                            t.wait_ge(sV.h, mRope)
                            t.wait_ge(sV.h, mXhat[sl])
                            t.wait_ge(sV.h, mStage)
                            for hh in range(8):
                                ins = t.transpose(pt2[0:64, hh * 128:(hh + 1) * 128],
                                                  qk_tm[:, hh * 64:(hh + 1) * 64],
                                                  eyeb[:])
                            sT.inc(ins)
                            mQKtr = sT.n
                            v.wait_ge(sT.h, mQKtr)
                            v.wait_ge(sSp.h, mSpillTot)
                            for hh in range(8):
                                h = blk * 8 + hh
                                sV.inc(v.tensor_copy(
                                    stage[(h % 2) * 64:(h % 2) * 64 + 64,
                                          (h // 2) * 128:(h // 2) * 128 + 128],
                                    pt2[0:64, hh * 128:(hh + 1) * 128]))
                            mStage = sV.n
                        else:
                            bi = ob - 2 * NB8
                            base = (seq * KT + tt) * (NH * 65) + bi * 8 * 65
                            dst = AP(v_sb2[:].tensor, base,
                                     [[BC * KT * NH * 65, 128], [65, 8], [1, 64]])
                            s.wait_ge(sT.h, mMM)
                            sS.inc(s.activation(out=dst,
                                                in_=pqs.rearrange("p (h d) -> p h d", d=64),
                                                func=AF.Copy))
                            mPQslot[ob % 2] = (sS.h, sS.n)
                            od = AP(v_sb2[:].tensor, base + 64,
                                    [[BC * KT * NH * 65, 128], [65, 8], [1, 1]])
                            v.wait_ge(sS.h, sS.n)
                            sV.inc(v.memset(od, 1.0))
                    g.wait_ge(sV.h, mStage)
                    for stage, dd in ((qstage, qfd), (kstage, kfd)):
                        for half in range(2):
                            src = stage[half * 64:(half + 1) * 64,
                                        0:(NH // 2) * 128].rearrange(
                                "d (a w) -> d a w", w=128)
                            dstv = dd[seq].rearrange("(a two) d w -> two a d w", two=2)[half] \
                                .transpose([1, 0, 2])
                            sSp.inc(g.dma_start(
                                out=dstv[:, :, tt * 128:(tt + 1) * 128], in_=src), 16)
                    mSpillTot = sSp.n
        mEndA1 = dict(t=sT.n, v=sV.n, s=sS.n)

    mEndA2 = {}
    with ExitStack() as a2st:
        a2sb = lambda n, sh, d: a2st.enter_context(nc.sbuf_tensor(n, sh, d))
        qfb = a2sb("qfb", [64, 2 * BC * S], BF16)
        kfb = a2sb("kfb", [64, 2 * BC * S], BF16)
        kbc = a2sb("kbc", [128, 8 * QHW], F32)
        kbtB = a2sb("kbtB", [128, 2 * QHW], F32)
        sbt = a2sb("sbt", [128, 2 * BC * QHW], F32)
        attnT = a2sb("attnT", [128, 2 * BC * QHW], BF16)
        mred = a2sb("mred", [64, 8], F32)
        bsc = a2sb("bsc", [1, 4], F32)
        mcol = a2sb("mcol", [128, 4], F32)
        zsb = a2sb("zsb", [1, QHW], F32)
        aot = a2sb("aot", [64, QHW], F32)
        aobuf = a2sb("aobuf", [64, 2 * QHW], BF16)
        with nc.psum_tensor("pbig", [128, 2048], F32) as pbig, \
             nc.psum_tensor("psB", [128, 2048], F32) as psB:
            ppvs = [pbig[0:65, 0:QHW], pbig[0:65, QHW:2 * QHW]]
            for eng in (q, v, s, t):
                eng.wait_ge(sT.h, mEndA1["t"])
                eng.wait_ge(sV.h, mEndA1["v"])
                eng.wait_ge(sS.h, mEndA1["s"])
            q.wait_ge(sSp.h, sSp.n)
            lastBias = [0, 0]
            lastEvac = [0, 0]
            sbtExp = {0: 0, 1: 0}
            lastPV = {}
            mZbLast = 0
            kbtBbias = {}
            cacheRead = [0] * 8
            cacheMark = [0] * 8
            mQKlast = [(None, 0), (None, 0)]
            mDot = [0, 0]
            for h in range(NH):
                sl = h % 2
                qof = sl * BC * S
                semq, valq = mQKlast[sl]
                if semq is not None:
                    q.wait_ge(semq, valq)
                for seq in range(BC):
                    sQF[sl].inc(q.dma_start(
                        out=qfb[0:64, qof + seq * S: qof + (seq + 1) * S],
                        in_=qfd[seq, h]), 16)
                    sKF[sl].inc(q.dma_start(
                        out=kfb[0:64, qof + seq * S: qof + (seq + 1) * S],
                        in_=kfd[seq, h]), 16)
                loadQ, loadK = sQF[sl].n, sKF[sl].n
                v.wait_ge(sQF[sl].h, loadQ)
                v.wait_ge(sKF[sl].h, loadK)
                v.wait_ge(sT.h, mDot[sl])
                for seq in range(BC):
                    sV.inc(v.reduce_max(out=mred[:, sl * 4 + 2 * seq: sl * 4 + 2 * seq + 1],
                                        in_=qfb[0:64, qof + seq * S: qof + (seq + 1) * S],
                                        axis=AX.X, apply_absolute_value=True))
                    sV.inc(v.reduce_max(out=mred[:, sl * 4 + 2 * seq + 1: sl * 4 + 2 * seq + 2],
                                        in_=kfb[0:64, qof + seq * S: qof + (seq + 1) * S],
                                        axis=AX.X, apply_absolute_value=True))
                mRed = sV.n
                t.wait_ge(sV.h, mRed)
                t.wait_ge(sV.h, lastEvac[0])
                for seq in range(BC):
                    sT.inc(t.matmul(pbig[0:1, seq:seq + 1],
                                    mred[0:64, sl * 4 + 2 * seq: sl * 4 + 2 * seq + 1],
                                    mred[0:64, sl * 4 + 2 * seq + 1: sl * 4 + 2 * seq + 2],
                                    start=True, stop=True))
                mDot[sl] = sT.n
                v.wait_ge(sT.h, mDot[sl])
                sV.inc(v.tensor_copy(bsc[:, sl * 2:sl * 2 + 2], pbig[0:1, 0:2]))
                v.wait_ge(sV.h, sV.n)
                sV.inc(v.tensor_tensor(out=bsc[:, sl * 2:sl * 2 + 1],
                                       in0=bsc[:, sl * 2:sl * 2 + 1],
                                       in1=bsc[:, sl * 2 + 1:sl * 2 + 2], op=ALU.max))
                mBsc = sV.n
                t.wait_ge(sV.h, mBsc)
                sT.inc(t.matmul(pbig[:, 0:1], onesf[0:1, 0:128],
                                bsc[0:1, sl * 2:sl * 2 + 1], start=True, stop=True))
                mPm = sT.n
                v.wait_ge(sT.h, mPm)
                sV.inc(v.tensor_scalar(out=mcol[:, sl:sl + 1], in0=pbig[:, 0:1],
                                       scalar1=-0.125, scalar2=None, op0=ALU.mult))
                mMcol = sV.n

                def kb_prep(qh, kt):
                    cached = (qh == 1 and kt >= 8)
                    tocache = (qh == 0 and kt < 8)
                    if cached:
                        slot = kt - 8
                        return kbc[:, slot * QHW:(slot + 1) * QHW], cacheMark[slot]
                    if tocache:
                        dst = kbc[:, kt * QHW:(kt + 1) * QHW]
                        war = cacheRead[kt]
                    else:
                        pB = kt % 2
                        dst = kbtB[:, pB * QHW:(pB + 1) * QHW]
                        war = kbtBbias.get(pB, 0)
                    v.wait_ge(sV.h, war)
                    if qh == 0 and kt >= 8:
                        sV.inc(v.tensor_scalar(
                            out=dst, in0=iotabn_s[:, 0:QHW],
                            scalar1=iotac_s[:, 0:1], scalar2=float(kt * 128),
                            op0=ALU.add, op1=ALU.add))
                    else:
                        q0 = qh * QHW
                        sV.inc(v.tensor_scalar(
                            out=dst, in0=iotab_s[:, q0:q0 + QHW],
                            scalar1=iotac_s[:, 0:1], scalar2=float(-kt * 128),
                            op0=ALU.subtract, op1=ALU.add))
                    mDelta = sV.n
                    s.wait_ge(sV.h, mDelta)
                    s.wait_ge(sV.h, war)
                    if tocache:
                        sS.inc(s.activation(out=dst, in_=dst, func=AF.Abs))
                        s.wait_ge(sS.h, sS.n)
                    sS.inc(s.activation(out=dst, in_=dst, func=AF.Ln,
                                        scale=pacols[:, 2 * h + 1:2 * h + 2],
                                        bias=1.0))
                    if tocache:
                        cacheMark[kt] = sS.n
                    return dst, sS.n

                def emit_evac(qh, pvstop):
                    nonlocal mZbLast
                    for seq in range(BC):
                        pA = (qh * BC + seq) % 2
                        v.wait_ge(sT.h, pvstop[seq])
                        sV.inc(v.tensor_copy(aot[:], ppvs[seq][0:64, :]))
                        mAot = sV.n
                        v.wait_ge(sT.h, mZbLast)
                        sV.inc(v.tensor_copy(zsb[:], ppvs[seq][64:65, :]))
                        mZc = sV.n
                        s.wait_ge(sV.h, mZc)
                        sS.inc(s.activation(out=zsb[:], in_=zsb[:], func=AF.Ln))
                        s.wait_ge(sS.h, sS.n)
                        sS.inc(s.activation(out=zsb[:], in_=zsb[:],
                                            func=AF.Exp, scale=-1.0))
                        mZinv = sS.n
                        t.wait_ge(sS.h, mZinv)
                        t.wait_ge(sV.h, mAot)
                        for half in range(2):
                            ins = t.matmul(
                                pbig[0:64, seq * QHW + half * 512:
                                     seq * QHW + (half + 1) * 512],
                                onesf[0:1, 0:64],
                                zsb[0:1, half * 512:(half + 1) * 512],
                                start=True, stop=True)
                        sT.inc(ins)
                        mZbLast = sT.n
                        v.wait_ge(sT.h, mZbLast)
                        v.wait_ge(sAO[pA].h, sAO[pA].n)
                        sV.inc(v.tensor_tensor(out=aobuf[:, pA * QHW:(pA + 1) * QHW],
                                               in0=aot[:],
                                               in1=pbig[0:64, seq * QHW:(seq + 1) * QHW],
                                               op=ALU.mult))
                        lastEvac[seq] = sV.n
                        q.wait_ge(sV.h, lastEvac[seq])
                        sAO[pA].inc(q.dma_start(
                            out=aod[seq, h * 64:(h + 1) * 64, qh * QHW:(qh + 1) * QHW],
                            in_=aobuf[:, pA * QHW:(pA + 1) * QHW]), 16)

                units = [(qh, kt) for qh in range(NQH) for kt in range(KT)]
                kbnext = kb_prep(*units[0])
                pend = []
                pvStop = [0, 0]

                def emit_pv(pqh, pkt, pexp, pp2):
                    t.wait_ge(sS.h, pexp)
                    if pkt == 0:
                        t.wait_ge(sV.h, lastEvac[0])
                        t.wait_ge(sV.h, lastEvac[1])
                        t.wait_ge(sV.h, mMcol)
                    for seq in range(BC):
                        vof = (seq * KT + pkt) * (NH * 65) + h * 65
                        ab = (seq * 2 + pp2) * QHW
                        for half in range(2):
                            ins = t.matmul(
                                ppvs[seq][:, half * 512:(half + 1) * 512],
                                v_sb2[:, vof:vof + 65],
                                attnT[:, ab + half * 512: ab + (half + 1) * 512],
                                start=(pkt == 0), stop=(pkt == KT - 1))
                        sT.inc(ins)
                        lastPV[seq * 2 + pp2] = sT.n
                        if pkt == KT - 1:
                            pvStop[seq] = sT.n
                    if pkt == KT - 1:
                        emit_evac(pqh, pvStop)

                for i, (qh, kt) in enumerate(units):
                    kbl, mLn = kbnext
                    kbnext = kb_prep(*units[i + 1]) if i + 1 < len(units) else None
                    p2 = kt % 2
                    mQKs = [0, 0]
                    t.wait_ge(sQF[sl].h, loadQ)
                    t.wait_ge(sKF[sl].h, loadK)
                    for seq in range(BC):
                        sof = qof + seq * S
                        t.wait_ge(sV.h, lastBias[seq])
                        for half in range(2):
                            ins = t.matmul(
                                psB[:, seq * QHW + half * 512:
                                    seq * QHW + (half + 1) * 512],
                                kfb[0:64, sof + kt * 128: sof + (kt + 1) * 128],
                                qfb[0:64, sof + qh * QHW + half * 512:
                                    sof + qh * QHW + (half + 1) * 512],
                                start=True, stop=True)
                        sT.inc(ins)
                        mQKs[seq] = sT.n
                    mQKlast[sl] = (sT.h, sT.n)
                    if len(pend) == 2:
                        emit_pv(*pend.pop(0))
                    for seq in range(BC):
                        v.wait_ge(sT.h, mQKs[seq])
                        v.wait_ge(sS.h, mLn)
                        v.wait_ge(sS.h, sbtExp[p2])
                        sV.inc(v.scalar_tensor_tensor(
                            out=sbt[:, (p2 * BC + seq) * QHW:(p2 * BC + seq + 1) * QHW],
                            in0=kbl, scalar=pacols[:, 2 * h:2 * h + 1],
                            in1=psB[:, seq * QHW:(seq + 1) * QHW],
                            op0=ALU.mult, op1=ALU.add))
                        lastBias[seq] = sV.n
                    cached = (qh == 1 and kt >= 8)
                    tocache = (qh == 0 and kt < 8)
                    if cached:
                        cacheRead[kt - 8] = sV.n
                    elif tocache:
                        cacheRead[kt] = sV.n
                    else:
                        kbtBbias[kt % 2] = sV.n
                    s.wait_ge(sV.h, lastBias[1])
                    s.wait_ge(sV.h, mMcol)
                    for seq in range(BC):
                        if seq * 2 + p2 in lastPV:
                            s.wait_ge(sT.h, lastPV[seq * 2 + p2])
                    aout = AP(attnT[:].tensor, p2 * QHW,
                              [[2 * BC * QHW, 128], [2 * QHW, BC], [1, QHW]])
                    sS.inc(s.activation(out=aout,
                                        in_=sbt[:, p2 * BC * QHW:(p2 + 1) * BC * QHW],
                                        func=AF.Exp, scale=0.125,
                                        bias=mcol[:, sl:sl + 1]))
                    sbtExp[p2] = sS.n
                    pend.append((qh, kt, sS.n, p2))
                for pe in pend:
                    emit_pv(*pe)
        mEndA2 = dict(t=sT.n, v=sV.n, s=sS.n)

    with ExitStack() as a4st:
        a4sb = lambda n, sh, d: a4st.enter_context(nc.sbuf_tensor(n, sh, d))
        FCA = NH * 64 // 128
        ao_fm = a4sb("ao_fm", [128, 2 * FCA * S], BF16)
        owt_sb = a4sb("owt_sb", [128, FCA * D], BF16)
        x_sb4 = a4sb("x_sb4", [128, 2 * D], F32)
        with nc.psum_tensor("pyo", [128, 2 * 1024], F32) as pyo:
            for eng in (g, q, t, v, s):
                eng.wait_ge(sT.h, mEndA2["t"])
                eng.wait_ge(sV.h, mEndA2["v"])
                eng.wait_ge(sS.h, mEndA2["s"])
            for sm in (sQF[0], sQF[1], sKF[0], sKF[1], sAO[0], sAO[1]):
                g.wait_ge(sm.h, sm.n)
                q.wait_ge(sm.h, sm.n)
            sOW.inc(g.dma_start(
                out=owt_sb[:].rearrange("p (f w) -> p f w", f=FCA),
                in_=out_wT.rearrange("(f p) o -> p f o", p=128)), 16)
            mVadd = [0, 0]
            for seq in range(BC):
                slf = seq % 2
                sAF[slf].inc(q.dma_start(
                    out=ao_fm[:, slf * FCA * S:(slf + 1) * FCA * S]
                        .rearrange("p (f w) -> p f w", f=FCA),
                    in_=aod[seq].rearrange("(f p) w -> p f w", p=128)), 16)
                for tt in range(KT):
                    sl = tt % 2
                    q.wait_ge(sEo[sl].h, sEo[sl].n)
                    q.wait_ge(sV.h, mVadd[sl])
                    sX[sl].inc(q.dma_start(
                        out=x_sb4[:, sl * D:(sl + 1) * D],
                        in_=x_src[seq * S + tt * 128: seq * S + (tt + 1) * 128, :]), 16)
                    pyb = pyo[:, sl * 1024: sl * 1024 + 1024]
                    t.wait_ge(sAF[slf].h, sAF[slf].n)
                    t.wait_ge(sOW.h, sOW.n)
                    t.wait_ge(sV.h, mVadd[sl])
                    for oh in range(2):
                        for j in range(FCA):
                            ins = t.matmul(
                                pyb[:, oh * 512: oh * 512 + 512],
                                ao_fm[:, slf * FCA * S + j * S + tt * 128:
                                      slf * FCA * S + j * S + (tt + 1) * 128],
                                owt_sb[:, j * D + oh * 512: j * D + oh * 512 + 512],
                                start=(j == 0), stop=(j == FCA - 1))
                        sT.inc(ins)
                    mMMa = sT.n
                    v.wait_ge(sT.h, mMMa)
                    v.wait_ge(sX[sl].h, sX[sl].n)
                    for oh in range(2):
                        et = x_sb4[:, sl * D + oh * 512: sl * D + oh * 512 + 512]
                        sV.inc(v.tensor_tensor(out=et, in0=et,
                                               in1=pyb[:, oh * 512: oh * 512 + 512],
                                               op=ALU.add))
                    mVadd[sl] = sV.n
                    q.wait_ge(sV.h, mVadd[sl])
                    sEo[sl].inc(q.dma_start(
                        out=x_dst[seq * S + tt * 128: seq * S + (tt + 1) * 128, :],
                        in_=x_sb4[:, sl * D:(sl + 1) * D]), 16)
    ast.close()




def _host_inputs(cfg, L0_attn_qkv_w, L0_attn_out_w, L0_attn_bias_p, L0_attn_bias_a,
                 L0_attn_rope_freqs, L0_ffn_w_router, L0_ffn_w1, L0_ffn_b1, L0_ffn_w2,
                 L0_ffn_b2):
    import ml_dtypes
    return {
        "qkv_wT": np.ascontiguousarray(np.asarray(L0_attn_qkv_w, np.float32).T),
        "out_wT": np.ascontiguousarray(np.asarray(L0_attn_out_w, np.float32).T),
        "bias_p": np.asarray(L0_attn_bias_p, np.float32).reshape(1, cfg.NH),
        "bias_a": np.asarray(L0_attn_bias_a, np.float32).reshape(1, cfg.NH),
        "rfreqs": np.asarray(L0_attn_rope_freqs, np.float32).reshape(1, cfg.HD // 2),
        "w_router": np.ascontiguousarray(np.asarray(L0_ffn_w_router, np.float32)),
        "w1": np.ascontiguousarray(
            np.asarray(L0_ffn_w1, np.float32)
            .reshape(cfg.E, 8, 128, cfg.HC, 128).transpose(0, 3, 2, 1, 4)
            .reshape(cfg.E, cfg.HC, 128, 8 * 128)),
        "b1l": np.ascontiguousarray(
            np.asarray(L0_ffn_b1, np.float32).reshape(cfg.E, cfg.HC, 128).transpose(2, 0, 1)),
        "w2": np.ascontiguousarray(np.asarray(L0_ffn_w2, np.float32)),
        "b2": np.ascontiguousarray(np.asarray(L0_ffn_b2, np.float32)),
        "iotab": np.tile(np.arange(cfg.S, dtype=np.float32)[None, :], (128, 1)),
        "iota_row": np.arange(max(cfg.S, 128), dtype=np.float32)[None, :],
        "eyeb": np.eye(128, dtype=ml_dtypes.bfloat16),
        "eyef": np.eye(128, dtype=np.float32),
    }


def kernel(L0_attn_x, L0_attn_qkv_w, L0_attn_out_w, L0_attn_bias_p, L0_attn_bias_a,
           L0_attn_rope_freqs, L0_ffn_w_router, L0_ffn_w1, L0_ffn_b1, L0_ffn_w2,
           L0_ffn_b2):
    cfg = Cfg()
    nc = build_program(cfg)
    x = np.ascontiguousarray(np.asarray(L0_attn_x, dtype=np.float32))
    shared = _host_inputs(cfg, L0_attn_qkv_w, L0_attn_out_w, L0_attn_bias_p,
                          L0_attn_bias_a, L0_attn_rope_freqs, L0_ffn_w_router,
                          L0_ffn_w1, L0_ffn_b1, L0_ffn_w2, L0_ffn_b2)
    in_maps = []
    for core in range(8):
        m = dict(shared)
        m["x_in"] = np.ascontiguousarray(x[2 * core:2 * core + 2].reshape(cfg.T, D))
        in_maps.append(m)
    res = run_bass_kernel_spmd(nc, in_maps, core_ids=list(range(8)))
    out = np.stack([r["y_out"].reshape(2, cfg.S, D) for r in res.results])
    return out.reshape(16, cfg.S, D)


# version 09
# speedup vs baseline: 1.1405x; 1.0213x over previous
import math
from contextlib import ExitStack

import numpy as np
import concourse.bass as bass
import concourse.mybir as mybir
from concourse.bass_utils import run_bass_kernel_spmd
from concourse.ap import AP

F32 = mybir.dt.float32
BF16 = mybir.dt.bfloat16
I32 = mybir.dt.int32
AF = mybir.ActivationFunctionType
ALU = mybir.AluOpType
AX = mybir.AxisListType

D = 1024
FC = D // 128


class Cfg:
    def __init__(self, T=4096, E=8, HID=4096, S=2048, NH=16, emit=(1, 2, 3), taps=()):
        self.T = T
        self.E = E
        self.HID = HID
        self.HC = HID // 128
        self.S = S
        self.BC = T // S
        self.NH = NH
        self.HD = 64
        self.QT = min(1024, T)
        self.NQ = T // self.QT
        self.NTQ = self.QT // 128
        self.TB = self.QT // 512
        self.KT = S // 128
        self.QHW = min(1024, S)
        self.NQH = S // self.QHW
        self.emit = emit
        self.taps = taps


class Sem:
    def __init__(self, nc, st, name, prime=0):
        self.h = st.enter_context(nc.semaphore(name))
        self.n = 0
        self.prime = prime
        if prime:
            nc.sync.sem_inc(self.h, prime)
            self.n = prime
    def inc(self, ins, k=1):
        ins.then_inc(self.h, k)
        self.n += k
        return ins


_RC_N = [0]
_NM_N = [0]
def _nm(n):
    _NM_N[0] += 1
    return f"{n}_{_NM_N[0]}"
class RC:
    def __init__(self, eng, name, init):
        self.eng = eng
        _RC_N[0] += 1
        self.r = eng.alloc_register(f"{name}_{_RC_N[0]}")
        eng.reg_mov(self.r, init)
    def use(self, step):
        r = self.r
        return r, step
    def wait(self, sem, step):
        self.eng.wait_ge(sem, self.r)
        if step:
            self.eng.reg_add(self.r, self.r, step)
    def bump(self, k):
        if k:
            self.eng.reg_add(self.r, self.r, k)


def build_program(cfg: Cfg):
    nc = bass.Bass("TRN2", target_bir_lowering=False, debug=False)
    c = cfg
    st = ExitStack()

    dt = nc.dram_tensor
    x_in = dt("x_in", [c.T, D], F32, kind="ExternalInput").ap()
    qkv_wT = dt("qkv_wT", [D, 3 * c.NH * c.HD], F32, kind="ExternalInput").ap()
    out_wT = dt("out_wT", [c.NH * c.HD, D], F32, kind="ExternalInput").ap()
    bias_p = dt("bias_p", [1, c.NH], F32, kind="ExternalInput").ap()
    bias_a = dt("bias_a", [1, c.NH], F32, kind="ExternalInput").ap()
    rfreqs = dt("rfreqs", [1, c.HD // 2], F32, kind="ExternalInput").ap()
    w_router = dt("w_router", [D, c.E], F32, kind="ExternalInput").ap()
    w1 = dt("w1", [c.E, c.HC, 128, FC * 128], F32, kind="ExternalInput").ap()
    b1l = dt("b1l", [128, c.E, c.HC], F32, kind="ExternalInput").ap()
    w2 = dt("w2", [c.E, c.HID, D], F32, kind="ExternalInput").ap()
    b2 = dt("b2", [c.E, D], F32, kind="ExternalInput").ap()
    iotab = dt("iotab", [128, c.S], F32, kind="ExternalInput").ap()
    iota_row = dt("iota_row", [1, max(c.S, 128)], F32, kind="ExternalInput").ap()
    eyeb_in = dt("eyeb", [128, 128], BF16, kind="ExternalInput").ap()
    eyef_in = dt("eyef", [128, 128], F32, kind="ExternalInput").ap()
    y_out = dt("y_out", [c.T, D], F32, kind="ExternalOutput").ap()

    def scratch(name, shape, dtp):
        kind = "ExternalOutput" if name in c.taps else "Internal"
        return dt(name, shape, dtp, kind=kind).ap()
    x1d = scratch("x1", [c.T, D], F32)
    x2d = scratch("x2", [c.T, D], F32)
    aod = scratch("ao", [c.BC, c.NH * c.HD, c.S], BF16)
    qfd = scratch("qf", [c.BC, c.NH, 64, c.S], BF16)
    kfd = scratch("kf", [c.BC, c.NH, 64, c.S], BF16)

    sb = lambda name, shape, dtp: st.enter_context(nc.sbuf_tensor(name, shape, dtp))
    eyeb = sb("eyeb_s", [128, 128], BF16)
    eyef = sb("eyef_s", [128, 128], F32)
    iotac_s = sb("iotac_s", [128, 1], F32)
    b1sb = sb("b1sb", [128, c.E * c.HC], F32)
    b2sb = sb("b2sb", [c.E, D], BF16)
    wrt_sb = sb("wrt_sb", [128, FC * c.E], BF16)
    pa_sb = sb("pa_sb", [1, 2 * c.NH], F32)
    freq_sb = sb("freq_sb", [1, c.HD // 2], F32)
    NT_S = c.S // 128
    cs_sb = sb("cs_sb", [128, NT_S * c.HD], F32)
    junk = sb("junk", [128, 512], F32)
    onesb = sb("onesb", [1, 128], BF16)
    epscol = sb("epscol", [128, 1], F32)

    S_ = lambda name, prime=0: Sem(nc, st, name, prime)
    sPre = S_("sPre")
    sPreG = S_("sPreG")
    sPos = S_("sPos")
    sV = S_("sV")
    sS = S_("sS")
    sT = S_("sT")
    sX = [S_("sX0"), S_("sX1")]
    sX2 = [S_("sX2a"), S_("sX2b")]
    NB = 4
    sW1 = [S_(f"sW1_{i}") for i in range(NB)]
    sW2 = [S_("sW2a"), S_("sW2b")]
    sB1 = [S_("sB1a"), S_("sB1b")]
    sWc = [S_("sWca"), S_("sWcb")]
    sMM = S_("sMM", prime=64)
    sG = S_("sG", prime=64)
    sF2 = S_("sF2", prime=64)
    sAcc = S_("sAcc", prime=64)
    sOut = S_("sOut")
    sEo = [S_("sEoA"), S_("sEoB")]

    q = nc.sync
    g = nc.gpsimd
    t = nc.tensor
    v = nc.vector
    s = nc.scalar
    dma_sems = [sPre, sPreG, sPos, sX[0], sX[1], sX2[0], sX2[1], sW1[0], sW1[1],
                sW1[2], sW1[3], sW2[0], sW2[1], sB1[0], sB1[1], sWc[0], sWc[1],
                sEo[0], sEo[1], sOut]

    def sublayer_entry():
        for eng in (q, g, t, v, s):
            for sm in dma_sems:
                eng.wait_ge(sm.h, sm.n)
            eng.wait_ge(sV.h, sV.n)
            eng.wait_ge(sS.h, sS.n)
            eng.wait_ge(sT.h, sT.n)

    sPre.inc(q.dma_start(out=eyeb[:], in_=eyeb_in), 16)
    sPre.inc(q.dma_start(out=eyef[:], in_=eyef_in), 16)
    sPre.inc(q.dma_start(out=iotac_s[:],
                         in_=AP(iota_row.tensor, 0, [[1, 128], [1, 1]])), 16)
    sPre.inc(q.dma_start(out=b1sb[:].rearrange("p (e h) -> p e h", e=c.E), in_=b1l), 16)
    sPre.inc(q.dma_start(out=pa_sb[:, 0:c.NH], in_=bias_p), 16)
    sPre.inc(q.dma_start(out=pa_sb[:, c.NH:2 * c.NH], in_=bias_a), 16)
    sPre.inc(q.dma_start(out=freq_sb[:], in_=rfreqs), 16)
    sPreG.inc(g.dma_start(out=b2sb[:], in_=b2), 16)
    sPreG.inc(g.dma_start(out=wrt_sb[:].rearrange("p (f e) -> p f e", f=FC),
                          in_=w_router.rearrange("(f p) e -> p f e", p=128)), 16)

    v.wait_ge(sPre.h, sPre.n)
    v.wait_ge(sPreG.h, sPreG.n)
    sV.inc(v.tensor_scalar(out=pa_sb[:], in0=pa_sb[:], scalar1=0.01, scalar2=None,
                           op0=ALU.max))
    sV.inc(v.memset(onesb[:], 1.0))
    sV.inc(v.memset(epscol[:], 1e-5))

    HD2 = c.HD // 2
    with nc.psum_tensor("pcs", [128, 512], F32) as pcs, \
         nc.sbuf_tensor("posrow", [1, max(c.S, 128)], F32) as posrow, \
         nc.sbuf_tensor("angf", [128, c.HD], F32) as angf, \
         nc.sbuf_tensor("angi", [128, c.HD], I32) as angi, \
         nc.sbuf_tensor("angr", [128, c.HD], F32) as angr:
        sPos.inc(q.dma_start(out=posrow[:], in_=iota_row), 16)
        for tt in range(NT_S):
            t.wait_ge(sPre.h, sPre.n)
            t.wait_ge(sPos.h, sPos.n)
            t.wait_ge(sV.h, sV.n)
            sT.inc(t.matmul(pcs[:, 0:HD2], posrow[:, tt * 128:(tt + 1) * 128],
                            freq_sb[:], start=True, stop=True))
            v.wait_ge(sT.h, sT.n)
            v.wait_ge(sS.h, sS.n)
            sV.inc(v.tensor_scalar(out=angf[:, 0:HD2], in0=pcs[:, 0:HD2],
                                   scalar1=float(np.pi / 2), scalar2=None, op0=ALU.add))
            sV.inc(v.tensor_copy(angf[:, HD2:c.HD], pcs[:, 0:HD2]))
            v.wait_ge(sV.h, sV.n)
            sV.inc(v.tensor_scalar(out=angr[:], in0=angf[:],
                                   scalar1=float(1.0 / (2 * np.pi)), scalar2=None,
                                   op0=ALU.mult))
            v.wait_ge(sV.h, sV.n)
            sV.inc(v.tensor_copy(angi[:], angr[:]))
            v.wait_ge(sV.h, sV.n)
            sV.inc(v.tensor_copy(angr[:], angi[:]))
            v.wait_ge(sV.h, sV.n)
            sV.inc(v.scalar_tensor_tensor(out=angf[:], in0=angr[:],
                                          scalar=float(-2 * np.pi), in1=angf[:],
                                          op0=ALU.mult, op1=ALU.add))
            v.wait_ge(sV.h, sV.n)
            sV.inc(v.tensor_scalar(out=angr[:], in0=angf[:], scalar1=float(np.pi),
                                   scalar2=None, op0=ALU.is_gt))
            v.wait_ge(sV.h, sV.n)
            sV.inc(v.scalar_tensor_tensor(out=angf[:], in0=angr[:],
                                          scalar=float(-2 * np.pi), in1=angf[:],
                                          op0=ALU.mult, op1=ALU.add))
            s.wait_ge(sV.h, sV.n)
            sS.inc(s.activation(out=cs_sb[:, tt * c.HD:(tt + 1) * c.HD], in_=angf[:],
                                func=AF.Sin))

    def ffn_sublayer(x_src, x_dst):
        E, HC, QT, NTQ, TB = c.E, c.HC, c.QT, c.NTQ, c.TB
        assert (E * HC) % NB == 0
        fst = ExitStack()
        fsb = lambda n, sh, d: fst.enter_context(nc.sbuf_tensor(_nm(n), sh, d))
        x_sb = fsb("x_sb", [128, 2 * D], F32)
        xhat_tm = fsb("xhat_tm", [128, D], BF16)
        xhat_fm = fsb("xhat_fm", [128, FC * QT], BF16)
        red = fsb("red", [128, 8], F32)
        w1buf = fsb("w1buf", [128, NB * D], BF16)
        w2_sb = fsb("w2_sb", [128, HC * D], BF16)
        g_sb = fsb("g_sb", [128, HC * QT], BF16)
        acc = fsb("acc", [128, NTQ * D], F32)
        wq_sb = fsb("wq_sb", [128, NTQ * E], F32)
        wtokT = fsb("wtokT", [c.E, QT], BF16)
        pexp = fsb("pexp", [128, E], F32)
        probs = fsb("probs", [128, E], F32)
        msk = fsb("msk", [128, E], F32)
        p2t = fsb("p2t", [128, E], F32)
        b1cur = fsb("b1cur", [128, 2 * HC], F32)
        wcur = fsb("wcur", [128, 2 * NTQ], F32)
        wq2 = fsb("wq2", [128, E * NTQ], F32)
        pst = ExitStack()
        ph0 = pst.enter_context(nc.psum_tensor(_nm("ph0"), [128, 512], F32))
        ph1 = pst.enter_context(nc.psum_tensor(_nm("ph1"), [128, 512], F32))
        po = pst.enter_context(nc.psum_tensor(_nm("po"), [128, 4 * 512], F32))
        pr = pst.enter_context(nc.psum_tensor(_nm("pr"), [128, 512], F32))
        pt = pst.enter_context(nc.psum_tensor(_nm("pt"), [128, 1024], BF16))
        phs = [ph0, ph1]

        sublayer_entry()
        for qq in range(c.NQ):
            tok0 = qq * QT
            v.wait_ge(sMM.h, sMM.n)
            for tt in range(NTQ):
                sl = tt % 2
                r0 = (tt % 2) * 4
                q.wait_ge(sV.h, sV.n)
                q.wait_ge(sS.h, sS.n)
                sX[sl].inc(q.dma_start(
                    out=x_sb[:, sl * D:(sl + 1) * D],
                    in_=x_src[tok0 + tt * 128: tok0 + (tt + 1) * 128, :]), 16)
                v.wait_ge(sX[sl].h, sX[sl].n)
                v.wait_ge(sT.h, sT.n)
                xs = x_sb[:, sl * D:(sl + 1) * D]
                sV.inc(v.reduce_sum(out=red[:, r0:r0 + 1], in_=xs, axis=AX.X, negate=True))
                v.wait_ge(sV.h, sV.n)
                sV.inc(v.tensor_scalar(out=red[:, r0 + 1:r0 + 2], in0=red[:, r0:r0 + 1],
                                       scalar1=float(1.0 / D), scalar2=None, op0=ALU.mult))
                v.wait_ge(sV.h, sV.n)
                sV.inc(v.tensor_scalar(out=xs, in0=xs, scalar1=red[:, r0 + 1:r0 + 2],
                                       scalar2=None, op0=ALU.add))
                s.wait_ge(sV.h, sV.n)
                sS.inc(s.activation(out=junk[:], in_=xs[:, 0:512], func=AF.Square,
                                    accum_out=red[:, r0 + 2:r0 + 3]))
                s.wait_ge(sS.h, sS.n)
                sS.inc(s.activation(out=junk[:], in_=xs[:, 512:1024], func=AF.Square,
                                    accum_out=red[:, r0 + 3:r0 + 4]))
                v.wait_ge(sS.h, sS.n)
                sV.inc(v.tensor_tensor(out=red[:, r0 + 2:r0 + 3], in0=red[:, r0 + 2:r0 + 3],
                                       in1=red[:, r0 + 3:r0 + 4], op=ALU.add))
                v.wait_ge(sV.h, sV.n)
                sV.inc(v.tensor_scalar(out=red[:, r0 + 3:r0 + 4], in0=red[:, r0 + 2:r0 + 3],
                                       scalar1=float(1.0 / D), scalar2=1e-5,
                                       op0=ALU.mult, op1=ALU.add))
                s.wait_ge(sV.h, sV.n)
                sS.inc(s.activation(out=red[:, r0 + 2:r0 + 3], in_=red[:, r0 + 3:r0 + 4],
                                    func=AF.Ln))
                s.wait_ge(sS.h, sS.n)
                sS.inc(s.activation(out=red[:, r0 + 3:r0 + 4], in_=red[:, r0 + 2:r0 + 3],
                                    func=AF.Exp, scale=-0.5))
                v.wait_ge(sS.h, sS.n)
                sV.inc(v.tensor_scalar(out=xhat_tm[:, 0:D], in0=xs,
                                       scalar1=red[:, r0 + 3:r0 + 4], scalar2=None,
                                       op0=ALU.mult))
                t.wait_ge(sV.h, sV.n)
                for j in range(FC):
                    ins = t.transpose(pt[:, j * 128:(j + 1) * 128],
                                      xhat_tm[:, j * 128:(j + 1) * 128],
                                      eyeb[:])
                sT.inc(ins)
                v.wait_ge(sT.h, sT.n)
                sV.inc(v.tensor_copy(
                    xhat_fm[:].rearrange("p (f w) -> p f w", f=FC)[:, :, tt * 128:(tt + 1) * 128],
                    pt[:].rearrange("p (f w) -> p f w", f=FC)))
                t.wait_ge(sV.h, sV.n)
                for j in range(FC):
                    ins = t.matmul(pr[:, 0:E],
                                   xhat_fm[:, j * QT + tt * 128: j * QT + (tt + 1) * 128],
                                   wrt_sb[:, j * E:(j + 1) * E],
                                   start=(j == 0), stop=(j == FC - 1))
                sT.inc(ins)
                v.wait_ge(sT.h, sT.n)
                sV.inc(v.reduce_max(out=red[:, r0 + 3:r0 + 4], in_=pr[:, 0:E], axis=AX.X,
                                    negate=True))
                s.wait_ge(sV.h, sV.n)
                sS.inc(s.activation(out=pexp[:], in_=pr[:, 0:E], func=AF.Exp,
                                    bias=red[:, r0 + 3:r0 + 4], accum_out=red[:, r0:r0 + 1]))
                v.wait_ge(sS.h, sS.n)
                sV.inc(v.reciprocal(out=red[:, r0 + 1:r0 + 2], in_=red[:, r0:r0 + 1]))
                v.wait_ge(sV.h, sV.n)
                sV.inc(v.tensor_scalar(out=probs[:], in0=pexp[:],
                                       scalar1=red[:, r0 + 1:r0 + 2], scalar2=None,
                                       op0=ALU.mult))
                v.wait_ge(sV.h, sV.n)
                sV.inc(v.reduce_max(out=red[:, r0:r0 + 1], in_=probs[:], axis=AX.X))
                v.wait_ge(sV.h, sV.n)
                sV.inc(v.tensor_scalar(out=msk[:], in0=probs[:], scalar1=red[:, r0:r0 + 1],
                                       scalar2=None, op0=ALU.is_ge))
                v.wait_ge(sV.h, sV.n)
                sV.inc(v.scalar_tensor_tensor(out=p2t[:], in0=msk[:], scalar=-1e30,
                                              in1=probs[:], op0=ALU.mult, op1=ALU.add))
                v.wait_ge(sV.h, sV.n)
                sV.inc(v.reduce_max(out=red[:, r0 + 1:r0 + 2], in_=p2t[:], axis=AX.X))
                v.wait_ge(sV.h, sV.n)
                sV.inc(v.tensor_scalar(out=msk[:], in0=probs[:],
                                       scalar1=red[:, r0 + 1:r0 + 2], scalar2=None,
                                       op0=ALU.is_ge))
                v.wait_ge(sV.h, sV.n)
                sV.inc(v.scalar_tensor_tensor(out=wq_sb[:, tt * E:(tt + 1) * E],
                                              in0=probs[:], scalar=0.5, in1=msk[:],
                                              op0=ALU.mult, op1=ALU.mult))
                v.wait_ge(sV.h, sV.n)
                sV.inc(v.tensor_copy(
                    AP(wq2[:].tensor, tt, [[E * NTQ, 128], [NTQ, E]]),
                    wq_sb[:, tt * E:(tt + 1) * E]))
                t.wait_ge(sV.h, sV.n)
                sT.inc(t.transpose(pr[0:E, 128:256], wq_sb[:, tt * E:(tt + 1) * E], eyef[:]))
                v.wait_ge(sT.h, sT.n)
                sV.inc(v.tensor_copy(wtokT[:, tt * 128:(tt + 1) * 128], pr[0:E, 128:256]))
            v.wait_ge(sV.h, sV.n)
            sV.inc(v.memset(acc[:], 0.0))

            f2base = sF2.n
            gbase = sG.n
            mmbase = sMM.n
            accbase = sAcc.n
            w1base = [sW1[j].n for j in range(NB)]
            w2base = [sW2[0].n, sW2[1].n]
            for e in range(E):
                sl_e = e % 2
                g.wait_ge(sG.h, gbase + max(0, (e - 1)) * (HC * TB) if e >= 2 else 0)
                sB1[sl_e].inc(g.dma_start(out=b1cur[:, sl_e * HC:(sl_e + 1) * HC],
                                          in_=b1l[:, e, :]), 16)
                g.wait_ge(sV.h, sV.n)
                g.wait_ge(sAcc.h, accbase + max(0, (e - 1)) * (NTQ * 2) if e >= 2 else 0)
                sWc[sl_e].inc(g.dma_start(out=wcur[:, sl_e * NTQ:(sl_e + 1) * NTQ],
                                          in_=wq2[:, e * NTQ:(e + 1) * NTQ]), 16)
                for hc in range(HC):
                    ci = e * HC + hc
                    sl = ci % NB
                    g.wait_ge(sG.h, gbase + (ci - NB + 1) * TB)
                    sW1[sl].inc(g.dma_start(
                        out=w1buf[:, sl * D:(sl + 1) * D],
                        in_=w1[e, hc]), 16)
                for ohh in range(2):
                    prev = (e - 1) * NTQ * 2 + (ohh + 1) * NTQ if e >= 1 else 0
                    g.wait_ge(sF2.h, f2base + prev)
                    sW2[ohh].inc(g.dma_start(
                        out=w2_sb[:, ohh * HC * 512:(ohh + 1) * HC * 512]
                            .rearrange("p (h w) -> p h w", h=HC),
                        in_=w2[e].rearrange("(h p) d -> p h d", p=128)
                            [:, :, ohh * 512:(ohh + 1) * 512]), 16)

            rW1 = [RC(t, f"rW1_{j}", w1base[j] + 16) for j in range(NB)]
            rG = RC(t, "rG", gbase - 1)
            rGE = RC(t, "rGE", gbase + HC * TB)
            rW2 = [RC(t, "rW2a", w2base[0] + 16), RC(t, "rW2b", w2base[1] + 16)]
            rAc = RC(t, "rAc", accbase - 3)
            with t.Fori(0, E) as ei:
                for gi in range(HC // NB):
                    for j in range(NB):
                        rW1[j].wait(sW1[j].h, 16)
                        for tb in range(TB):
                            k = (gi * NB + j) * TB + tb
                            rG.wait(sG.h, 1)
                            ps = phs[k % 2]
                            for fc_ in range(FC):
                                ins = t.matmul(
                                    ps[:],
                                    w1buf[:, j * D + fc_ * 128: j * D + (fc_ + 1) * 128],
                                    xhat_fm[:, fc_ * QT + tb * 512: fc_ * QT + tb * 512 + 512],
                                    start=(fc_ == 0), stop=(fc_ == FC - 1))
                            ins.then_inc(sMM.h, 1)
                rGE.wait(sG.h, HC * TB)
                for oh in range(2):
                    rW2[oh].wait(sW2[oh].h, 16)
                    for ts in range(NTQ):
                        g2 = oh * NTQ + ts
                        rAc.wait(sAcc.h, 1)
                        pob = po[:, (g2 % 4) * 512:(g2 % 4) * 512 + 512]
                        for hc in range(HC):
                            ins = t.matmul(
                                pob,
                                g_sb[:, hc * QT + ts * 128: hc * QT + (ts + 1) * 128],
                                w2_sb[:, oh * HC * 512 + hc * 512:
                                      oh * HC * 512 + (hc + 1) * 512],
                                start=(hc == 0), stop=(hc == HC - 1))
                        ins.then_inc(sF2.h, 1)
            for j in range(NB):
                t.free_register(rW1[j].r)
            for rc in (rG, rGE, rW2[0], rW2[1], rAc):
                t.free_register(rc.r)
            sMM.n = mmbase + E * HC * TB
            sF2.n = f2base + E * NTQ * 2

            rF2s = RC(s, "rF2s", f2base)
            rMMs = RC(s, "rMMs", mmbase + 1)
            rB1w = [RC(s, "rB1w0", sB1[0].n - (E // 2 - 1) * 16),
                    RC(s, "rB1w1", sB1[1].n - (E // 2 - 1) * 16)]
            with s.Fori(0, E // 2) as ei:
                for sl_e in range(2):
                    rF2s.wait(sF2.h, NTQ * 2)
                    rB1w[sl_e].wait(sB1[sl_e].h, 16)
                    for gi in range(HC // NB):
                        for j in range(NB):
                            hcq = gi * NB + j
                            for tb in range(TB):
                                k = hcq * TB + tb
                                rMMs.wait(sMM.h, 1)
                                ps = phs[k % 2]
                                dst = AP(g_sb[:].tensor, hcq * QT + tb * 512,
                                         [[HC * QT, 128], [1, 512]])
                                s.activation(out=dst, in_=ps[:], func=AF.Gelu_apprx_tanh,
                                             bias=b1cur[:, sl_e * HC + hcq: sl_e * HC + hcq + 1]
                                             ).then_inc(sG.h, 1)
            for rc in (rF2s, rMMs, rB1w[0], rB1w[1]):
                s.free_register(rc.r)
            sG.n = gbase + E * HC * TB

            v.wait_ge(sV.h, sV.n)
            rF2v = RC(v, "rF2v", f2base + 1)
            rAcv = RC(v, "rAcv", accbase - (NTQ * 2) + 1)
            rWcw = [RC(v, "rWcw0", sWc[0].n - (E // 2 - 1) * 16),
                    RC(v, "rWcw1", sWc[1].n - (E // 2 - 1) * 16)]
            with v.Fori(0, E // 2) as ei:
                for sl_e in range(2):
                    rWcw[sl_e].wait(sWc[sl_e].h, 16)
                    for oh in range(2):
                        for ts in range(NTQ):
                            rF2v.wait(sF2.h, 1)
                            rAcv.wait(sAcc.h, 1)
                            g2 = oh * NTQ + ts
                            pob = po[:, (g2 % 4) * 512:(g2 % 4) * 512 + 512]
                            at = acc[:, ts * D + oh * 512: ts * D + oh * 512 + 512]
                            wptr = wcur[:, sl_e * NTQ + ts: sl_e * NTQ + ts + 1]
                            v.scalar_tensor_tensor(out=at, in0=pob, scalar=wptr, in1=at,
                                                   op0=ALU.mult, op1=ALU.add).then_inc(sAcc.h, 1)
            for rc in (rF2v, rAcv, rWcw[0], rWcw[1]):
                v.free_register(rc.r)
            sAcc.n = accbase + E * NTQ * 2

            eobase = [sEo[0].n, sEo[1].n]
            for ts in range(NTQ):
                sl = ts % 2
                q.wait_ge(sV.h, sV.n)
                q.wait_ge(sS.h, sS.n)
                if ts >= 2:
                    q.wait_ge(sEo[sl].h, eobase[sl] + (ts // 2) * 16)
                sX2[sl].inc(q.dma_start(
                    out=x_sb[:, sl * D:(sl + 1) * D],
                    in_=x_src[tok0 + ts * 128: tok0 + (ts + 1) * 128, :]), 16)
                t.wait_ge(sPreG.h, sPreG.n)
                t.wait_ge(sAcc.h, sAcc.n)
                for oh in range(2):
                    t.wait_ge(sV.h, sV.n)
                    sT.inc(t.matmul(pr[:, 0:512], wtokT[:, ts * 128:(ts + 1) * 128],
                                    b2sb[:, oh * 512:(oh + 1) * 512], start=True, stop=True))
                    v.wait_ge(sT.h, sT.n)
                    v.wait_ge(sX2[sl].h, sX2[sl].n)
                    et = x_sb[:, sl * D + oh * 512: sl * D + oh * 512 + 512]
                    sV.inc(v.tensor_tensor(out=et, in0=et, in1=pr[:, 0:512], op=ALU.add))
                    v.wait_ge(sV.h, sV.n)
                    at = acc[:, ts * D + oh * 512: ts * D + oh * 512 + 512]
                    sV.inc(v.tensor_tensor(out=et, in0=et, in1=at, op=ALU.add))
                q.wait_ge(sV.h, sV.n)
                sEo[sl].inc(q.dma_start(out=x_dst[tok0 + ts * 128: tok0 + (ts + 1) * 128, :],
                                        in_=x_sb[:, sl * D:(sl + 1) * D]), 16)
        pst.close()
        fst.close()

    def attn_sublayer(x_src, x_dst):
        _emit_attention(nc, c, x_src, x_dst, dict(
            qkv_wT=qkv_wT, out_wT=out_wT, aod=aod, qfd=qfd, kfd=kfd, iotab=iotab,
            eyeb=eyeb, eyef=eyef, iotac_s=iotac_s, pa_sb=pa_sb, epscol=epscol,
            cs_sb=cs_sb, onesb=onesb, junk=junk, sPos=sPos,
            dma_sems=dma_sems, sublayer_entry=sublayer_entry,
            sems=dict(sPre=sPre, sV=sV, sS=sS, sT=sT, sX=sX, sX2=sX2, sOut=sOut,
                      sEo=sEo),
            S_=S_))

    if 1 in c.emit:
        dst1 = x1d if (2 in c.emit or 3 in c.emit or "x1" in c.taps) else y_out
        ffn_sublayer(x_in, dst1)
    if 2 in c.emit:
        src2 = x1d if 1 in c.emit else x_in
        dst2 = x2d if (3 in c.emit or "x2" in c.taps) else y_out
        attn_sublayer(src2, dst2)
    if 3 in c.emit:
        src3 = x2d if 2 in c.emit else (x1d if 1 in c.emit else x_in)
        ffn_sublayer(src3, y_out)

    q.wait_ge(sOut.h, sOut.n)
    q.wait_ge(sEo[0].h, sEo[0].n)
    q.wait_ge(sEo[1].h, sEo[1].n)
    st.close()
    return nc


def _emit_attention(nc, c, x_src, x_dst, env):
    _attn_emit(nc, c, x_src, x_dst, env)


def _attn_emit(nc, c, x_src, x_dst, env):
    qkv_wT, out_wT = env["qkv_wT"], env["out_wT"]
    aod, qfd, kfd, iotab = env["aod"], env["qfd"], env["kfd"], env["iotab"]
    eyeb, eyef, iotac_s, pa_sb = env["eyeb"], env["eyef"], env["iotac_s"], env["pa_sb"]
    cs_sb, onesb, junk = env["cs_sb"], env["onesb"], env["junk"]
    epscol = env["epscol"]
    sems = env["sems"]
    sPre, sV, sS, sT = sems["sPre"], sems["sV"], sems["sS"], sems["sT"]
    sX, sEo = sems["sX"], sems["sEo"]
    S_ = env["S_"]

    q = nc.sync
    g = nc.gpsimd
    t = nc.tensor
    v = nc.vector
    s = nc.scalar

    S, NH, KT, BC = c.S, c.NH, c.KT, c.BC
    QHW, NQH = c.QHW, c.NQH
    NO = 3 * NH * 64 // 512
    NB8 = NH // 8

    sQV = S_("sQV")
    sSp = S_("sSp")
    sQF = [S_("sQF0"), S_("sQF1")]
    sKF = [S_("sKF0"), S_("sKF1")]
    sAO = [S_("sAOa"), S_("sAOb")]
    sAF = [S_("sAFa"), S_("sAFb")]
    sOW = S_("sOW")
    env["dma_sems"].extend([sQV, sSp, sQF[0], sQF[1], sKF[0], sKF[1],
                            sAO[0], sAO[1], sAF[0], sAF[1], sOW])

    ast = ExitStack()
    asb = lambda n, sh, d: ast.enter_context(nc.sbuf_tensor(n, sh, d))
    iotab_s = asb("iotab_s", [128, S], F32)
    iotabn_s = asb("iotabn_s", [128, S], F32)
    v_sb2 = asb("v_sb2", [128, BC * KT * (NH * 65)], BF16)
    pacols = asb("pacols", [128, 2 * NH], F32)
    onesf = asb("onesf", [1, 128], F32)
    env["sublayer_entry"]()
    sPre.inc(q.dma_start(out=iotab_s[:], in_=iotab), 16)

    v.wait_ge(sPre.h, sPre.n)
    sV.inc(v.memset(onesf[:], 1.0))
    v.wait_ge(sV.h, sV.n)
    sV.inc(v.tensor_scalar(out=iotabn_s[:], in0=iotab_s[:], scalar1=-1.0,
                           scalar2=None, op0=ALU.mult))

    with nc.psum_tensor("ppa", [128, 512], F32) as ppa:
        for h in range(NH):
            t.wait_ge(sV.h, sV.n)
            t.wait_ge(sPre.h, sPre.n)
            sT.inc(t.matmul(ppa[:, 0:1], onesf[:], pa_sb[0:1, h:h + 1],
                            start=True, stop=True))
            sT.inc(t.matmul(ppa[:, 1:2], onesf[:], pa_sb[0:1, NH + h:NH + h + 1],
                            start=True, stop=True))
            v.wait_ge(sT.h, sT.n)
            sV.inc(v.tensor_scalar(out=pacols[:, 2 * h:2 * h + 1], in0=ppa[:, 0:1],
                                   scalar1=-8.0, scalar2=None, op0=ALU.mult))
            sV.inc(v.tensor_copy(pacols[:, 2 * h + 1:2 * h + 2], ppa[:, 1:2]))

    mEndA1 = {}
    with ExitStack() as a1st:
        a1sb = lambda n, sh, d: a1st.enter_context(nc.sbuf_tensor(n, sh, d))
        x_sb = a1sb("x_sb_a", [128, 2 * D], F32)
        junka = a1sb("junka", [128, D], F32)
        xhat_tm = a1sb("xhat_tm_a", [128, D], BF16)
        xt_fm = a1sb("xt_fm", [128, 2 * D], BF16)
        red = a1sb("red_a", [128, 12], F32)
        qkvw_sb = a1sb("qkvw_sb", [128, NO * FC * 512], BF16)
        crep = a1sb("crep", [128, 512], F32)
        ropeA = a1sb("ropeA", [128, 256], F32)
        ropeB = a1sb("ropeB", [128, 256], F32)
        qk_tm = a1sb("qk_tm", [128, 512], BF16)
        qstage = a1sb("qstage", [128, 2048], BF16)
        kstage = a1sb("kstage", [128, 2048], BF16)
        with nc.psum_tensor("pq", [128, 2 * 512], F32) as pq, \
             nc.psum_tensor("pt2", [128, 1024], BF16) as pt2:

            g.wait_ge(sPre.h, sPre.n)
            for ob in range(NO):
                sQV.inc(g.dma_start(
                    out=qkvw_sb[:, ob * FC * 512:(ob + 1) * FC * 512]
                        .rearrange("p (f w) -> p f w", f=FC),
                    in_=qkv_wT.rearrange("(f p) o -> p f o", p=128)
                        [:, :, ob * 512:(ob + 1) * 512]), 16)

            mLNx = [0, 0]
            mA0tr = 0
            mXhat = [0, 0]
            mXuse = [0, 0]
            mPQslot = {0: (None, 0), 1: (None, 0)}
            mRope = 0
            mQKtr = 0
            mStage = 0
            mSpillTot = 0
            for seq in range(BC):
                base_tok = seq * S
                for tt in range(KT):
                    sl = tt % 2
                    r0 = 6 * sl
                    q.wait_ge(sV.h, mLNx[sl])
                    sX[sl].inc(q.dma_start(
                        out=x_sb[:, sl * D:(sl + 1) * D],
                        in_=x_src[base_tok + tt * 128: base_tok + (tt + 1) * 128, :]), 16)
                    xs = x_sb[:, sl * D:(sl + 1) * D]
                    s.wait_ge(sX[sl].h, sX[sl].n)
                    sS.inc(s.activation(out=junka[:], in_=xs, func=AF.Copy,
                                        accum_out=red[:, r0:r0 + 1]))
                    s.wait_ge(sS.h, sS.n)
                    sS.inc(s.activation(out=red[:, r0 + 1:r0 + 2], in_=red[:, r0:r0 + 1],
                                        func=AF.Copy, scale=float(-1.0 / D)))
                    s.wait_ge(sS.h, sS.n)
                    sS.inc(s.activation(out=junka[:], in_=xs, func=AF.Square,
                                        bias=red[:, r0 + 1:r0 + 2],
                                        accum_out=red[:, r0 + 2:r0 + 3]))
                    s.wait_ge(sS.h, sS.n)
                    sS.inc(s.activation(out=red[:, r0 + 3:r0 + 4], in_=red[:, r0 + 2:r0 + 3],
                                        func=AF.Ln, scale=float(1.0 / D),
                                        bias=epscol[:, 0:1]))
                    s.wait_ge(sS.h, sS.n)
                    sS.inc(s.activation(out=red[:, r0 + 4:r0 + 5], in_=red[:, r0 + 3:r0 + 4],
                                        func=AF.Exp, scale=-0.5))
                    v.wait_ge(sS.h, sS.n)
                    v.wait_ge(sT.h, mA0tr)
                    sV.inc(v.tensor_scalar(out=xhat_tm[:], in0=xs,
                                           scalar1=red[:, r0 + 1:r0 + 2],
                                           scalar2=red[:, r0 + 4:r0 + 5],
                                           op0=ALU.add, op1=ALU.mult))
                    mLNx[sl] = sV.n
                    t.wait_ge(sV.h, mLNx[sl])
                    t.wait_ge(sV.h, mStage)
                    for j in range(FC):
                        ins = t.transpose(pt2[:, j * 128:(j + 1) * 128],
                                          xhat_tm[:, j * 128:(j + 1) * 128], eyeb[:])
                    sT.inc(ins)
                    mA0tr = sT.n
                    v.wait_ge(sT.h, mA0tr)
                    v.wait_ge(sT.h, mXuse[sl])
                    sV.inc(v.tensor_copy(xt_fm[:, sl * D:(sl + 1) * D], pt2[:]))
                    mXhat[sl] = sV.n
                    v.wait_ge(sV.h, mRope)
                    for rr in range(8):
                        sV.inc(v.tensor_copy(crep[:, rr * 32:(rr + 1) * 32],
                                             cs_sb[:, tt * 64: tt * 64 + 32]))
                        sV.inc(v.tensor_copy(crep[:, 256 + rr * 32: 256 + (rr + 1) * 32],
                                             cs_sb[:, tt * 64 + 32: tt * 64 + 64]))
                    for ob in range(NO):
                        pqs = pq[:, (ob % 2) * 512:(ob % 2) * 512 + 512]
                        t.wait_ge(sQV.h, sQV.n)
                        t.wait_ge(sV.h, mXhat[sl])
                        sem_pq, val_pq = mPQslot[ob % 2]
                        if sem_pq is not None:
                            t.wait_ge(sem_pq, val_pq)
                        for j in range(FC):
                            ins = t.matmul(
                                pqs,
                                xt_fm[:, sl * D + j * 128: sl * D + (j + 1) * 128],
                                qkvw_sb[:, ob * FC * 512 + j * 512:
                                        ob * FC * 512 + (j + 1) * 512],
                                start=(j == 0), stop=(j == FC - 1))
                        sT.inc(ins)
                        mMM = sT.n
                        if ob == NO - 1:
                            mXuse[sl] = sT.n
                        if ob < 2 * NB8:
                            is_q = ob < NB8
                            blk = ob if is_q else ob - NB8
                            pv = pqs.rearrange("p (h two d) -> p h two d", two=2, d=32)
                            t1v, t2v = pv[:, :, 0, :], pv[:, :, 1, :]
                            cr = crep[:, 0:256].rearrange("p (h d) -> p h d", d=32)
                            sr = crep[:, 256:512].rearrange("p (h d) -> p h d", d=32)
                            qv = qk_tm[:].rearrange("p (h two d) -> p h two d", two=2, d=32)
                            rA = ropeA[:].rearrange("p (h d) -> p h d", d=32)
                            rB = ropeB[:].rearrange("p (h d) -> p h d", d=32)
                            v.wait_ge(sT.h, mMM)
                            v.wait_ge(sT.h, mQKtr)
                            sV.inc(v.tensor_tensor(out=rA, in0=t1v, in1=cr, op=ALU.mult))
                            sV.inc(v.tensor_tensor(out=rB, in0=t2v, in1=sr, op=ALU.mult))
                            v.wait_ge(sV.h, sV.n)
                            sV.inc(v.tensor_tensor(out=qv[:, :, 0, :], in0=rA, in1=rB,
                                                   op=ALU.subtract))
                            v.wait_ge(sV.h, sV.n)
                            sV.inc(v.tensor_tensor(out=rA, in0=t1v, in1=sr, op=ALU.mult))
                            sV.inc(v.tensor_tensor(out=rB, in0=t2v, in1=cr, op=ALU.mult))
                            v.wait_ge(sV.h, sV.n)
                            sV.inc(v.tensor_tensor(out=qv[:, :, 1, :], in0=rA, in1=rB,
                                                   op=ALU.add))
                            mRope = sV.n
                            mPQslot[ob % 2] = (sV.h, sV.n)
                            stage = qstage if is_q else kstage
                            t.wait_ge(sV.h, mRope)
                            t.wait_ge(sV.h, mXhat[sl])
                            t.wait_ge(sV.h, mStage)
                            for hh in range(8):
                                ins = t.transpose(pt2[0:64, hh * 128:(hh + 1) * 128],
                                                  qk_tm[:, hh * 64:(hh + 1) * 64],
                                                  eyeb[:])
                            sT.inc(ins)
                            mQKtr = sT.n
                            v.wait_ge(sT.h, mQKtr)
                            v.wait_ge(sSp.h, mSpillTot)
                            for hh in range(8):
                                h = blk * 8 + hh
                                sV.inc(v.tensor_copy(
                                    stage[(h % 2) * 64:(h % 2) * 64 + 64,
                                          (h // 2) * 128:(h // 2) * 128 + 128],
                                    pt2[0:64, hh * 128:(hh + 1) * 128]))
                            mStage = sV.n
                        else:
                            bi = ob - 2 * NB8
                            base = (seq * KT + tt) * (NH * 65) + bi * 8 * 65
                            dst = AP(v_sb2[:].tensor, base,
                                     [[BC * KT * NH * 65, 128], [65, 8], [1, 64]])
                            s.wait_ge(sT.h, mMM)
                            sS.inc(s.activation(out=dst,
                                                in_=pqs.rearrange("p (h d) -> p h d", d=64),
                                                func=AF.Copy))
                            mPQslot[ob % 2] = (sS.h, sS.n)
                            od = AP(v_sb2[:].tensor, base + 64,
                                    [[BC * KT * NH * 65, 128], [65, 8], [1, 1]])
                            v.wait_ge(sS.h, sS.n)
                            sV.inc(v.memset(od, 1.0))
                    g.wait_ge(sV.h, mStage)
                    for stage, dd in ((qstage, qfd), (kstage, kfd)):
                        for half in range(2):
                            src = stage[half * 64:(half + 1) * 64,
                                        0:(NH // 2) * 128].rearrange(
                                "d (a w) -> d a w", w=128)
                            dstv = dd[seq].rearrange("(a two) d w -> two a d w", two=2)[half] \
                                .transpose([1, 0, 2])
                            sSp.inc(g.dma_start(
                                out=dstv[:, :, tt * 128:(tt + 1) * 128], in_=src), 16)
                    mSpillTot = sSp.n
        mEndA1 = dict(t=sT.n, v=sV.n, s=sS.n)

    mEndA2 = {}
    with ExitStack() as a2st:
        a2sb = lambda n, sh, d: a2st.enter_context(nc.sbuf_tensor(n, sh, d))
        qfb = a2sb("qfb", [64, 2 * BC * S], BF16)
        kfb = a2sb("kfb", [64, 2 * BC * S], BF16)
        kbc = a2sb("kbc", [128, 8 * QHW], F32)
        kbtB = a2sb("kbtB", [128, 2 * QHW], F32)
        sbt = a2sb("sbt", [128, 2 * BC * QHW], F32)
        attnT = a2sb("attnT", [128, 2 * BC * QHW], BF16)
        mred = a2sb("mred", [64, 8], F32)
        bsc = a2sb("bsc", [1, 4], F32)
        mcol = a2sb("mcol", [128, 4], F32)
        zsb = a2sb("zsb", [1, QHW], F32)
        aot = a2sb("aot", [64, QHW], F32)
        aobuf = a2sb("aobuf", [64, 2 * QHW], BF16)
        with nc.psum_tensor("pbig", [128, 2048], F32) as pbig, \
             nc.psum_tensor("psB", [128, 2048], F32) as psB:
            ppvs = [pbig[0:65, 0:QHW], pbig[0:65, QHW:2 * QHW]]
            for eng in (q, v, s, t):
                eng.wait_ge(sT.h, mEndA1["t"])
                eng.wait_ge(sV.h, mEndA1["v"])
                eng.wait_ge(sS.h, mEndA1["s"])
            q.wait_ge(sSp.h, sSp.n)
            lastBias = [0, 0]
            lastEvac = [0, 0]
            sbtExp = {0: 0, 1: 0}
            lastPV = {}
            mZbLast = 0
            kbtBbias = {}
            cacheRead = [0] * 8
            cacheMark = [0] * 8
            mQKlast = [(None, 0), (None, 0)]
            mDot = [0, 0]
            for h in range(NH):
                sl = h % 2
                qof = sl * BC * S
                semq, valq = mQKlast[sl]
                if semq is not None:
                    q.wait_ge(semq, valq)
                for seq in range(BC):
                    sQF[sl].inc(q.dma_start(
                        out=qfb[0:64, qof + seq * S: qof + (seq + 1) * S],
                        in_=qfd[seq, h]), 16)
                    sKF[sl].inc(q.dma_start(
                        out=kfb[0:64, qof + seq * S: qof + (seq + 1) * S],
                        in_=kfd[seq, h]), 16)
                loadQ, loadK = sQF[sl].n, sKF[sl].n
                v.wait_ge(sQF[sl].h, loadQ)
                v.wait_ge(sKF[sl].h, loadK)
                v.wait_ge(sT.h, mDot[sl])
                for seq in range(BC):
                    sV.inc(v.reduce_max(out=mred[:, sl * 4 + 2 * seq: sl * 4 + 2 * seq + 1],
                                        in_=qfb[0:64, qof + seq * S: qof + (seq + 1) * S],
                                        axis=AX.X, apply_absolute_value=True))
                    sV.inc(v.reduce_max(out=mred[:, sl * 4 + 2 * seq + 1: sl * 4 + 2 * seq + 2],
                                        in_=kfb[0:64, qof + seq * S: qof + (seq + 1) * S],
                                        axis=AX.X, apply_absolute_value=True))
                mRed = sV.n
                t.wait_ge(sV.h, mRed)
                t.wait_ge(sV.h, lastEvac[0])
                for seq in range(BC):
                    sT.inc(t.matmul(pbig[0:1, seq:seq + 1],
                                    mred[0:64, sl * 4 + 2 * seq: sl * 4 + 2 * seq + 1],
                                    mred[0:64, sl * 4 + 2 * seq + 1: sl * 4 + 2 * seq + 2],
                                    start=True, stop=True))
                mDot[sl] = sT.n
                v.wait_ge(sT.h, mDot[sl])
                sV.inc(v.tensor_copy(bsc[:, sl * 2:sl * 2 + 2], pbig[0:1, 0:2]))
                v.wait_ge(sV.h, sV.n)
                sV.inc(v.tensor_tensor(out=bsc[:, sl * 2:sl * 2 + 1],
                                       in0=bsc[:, sl * 2:sl * 2 + 1],
                                       in1=bsc[:, sl * 2 + 1:sl * 2 + 2], op=ALU.max))
                mBsc = sV.n
                t.wait_ge(sV.h, mBsc)
                sT.inc(t.matmul(pbig[:, 0:1], onesf[0:1, 0:128],
                                bsc[0:1, sl * 2:sl * 2 + 1], start=True, stop=True))
                mPm = sT.n
                v.wait_ge(sT.h, mPm)
                sV.inc(v.tensor_scalar(out=mcol[:, sl:sl + 1], in0=pbig[:, 0:1],
                                       scalar1=-0.125, scalar2=None, op0=ALU.mult))
                mMcol = sV.n

                def kb_prep(qh, kt):
                    cached = (qh == 1 and kt >= 8)
                    tocache = (qh == 0 and kt < 8)
                    if cached:
                        slot = kt - 8
                        return kbc[:, slot * QHW:(slot + 1) * QHW], cacheMark[slot]
                    if tocache:
                        dst = kbc[:, kt * QHW:(kt + 1) * QHW]
                        war = cacheRead[kt]
                    else:
                        pB = kt % 2
                        dst = kbtB[:, pB * QHW:(pB + 1) * QHW]
                        war = kbtBbias.get(pB, 0)
                    v.wait_ge(sV.h, war)
                    if qh == 0 and kt >= 8:
                        sV.inc(v.tensor_scalar(
                            out=dst, in0=iotabn_s[:, 0:QHW],
                            scalar1=iotac_s[:, 0:1], scalar2=float(kt * 128),
                            op0=ALU.add, op1=ALU.add))
                    else:
                        q0 = qh * QHW
                        sV.inc(v.tensor_scalar(
                            out=dst, in0=iotab_s[:, q0:q0 + QHW],
                            scalar1=iotac_s[:, 0:1], scalar2=float(-kt * 128),
                            op0=ALU.subtract, op1=ALU.add))
                    mDelta = sV.n
                    s.wait_ge(sV.h, mDelta)
                    s.wait_ge(sV.h, war)
                    if tocache:
                        sS.inc(s.activation(out=dst, in_=dst, func=AF.Abs))
                        s.wait_ge(sS.h, sS.n)
                    sS.inc(s.activation(out=dst, in_=dst, func=AF.Ln,
                                        scale=pacols[:, 2 * h + 1:2 * h + 2],
                                        bias=1.0))
                    if tocache:
                        cacheMark[kt] = sS.n
                    return dst, sS.n

                def emit_evac(qh, pvstop):
                    nonlocal mZbLast
                    for seq in range(BC):
                        pA = (qh * BC + seq) % 2
                        v.wait_ge(sT.h, pvstop[seq])
                        sV.inc(v.tensor_copy(aot[:], ppvs[seq][0:64, :]))
                        mAot = sV.n
                        v.wait_ge(sT.h, mZbLast)
                        sV.inc(v.tensor_copy(zsb[:], ppvs[seq][64:65, :]))
                        mZc = sV.n
                        s.wait_ge(sV.h, mZc)
                        sS.inc(s.activation(out=zsb[:], in_=zsb[:], func=AF.Ln))
                        s.wait_ge(sS.h, sS.n)
                        sS.inc(s.activation(out=zsb[:], in_=zsb[:],
                                            func=AF.Exp, scale=-1.0))
                        mZinv = sS.n
                        t.wait_ge(sS.h, mZinv)
                        t.wait_ge(sV.h, mAot)
                        for half in range(2):
                            ins = t.matmul(
                                pbig[0:64, seq * QHW + half * 512:
                                     seq * QHW + (half + 1) * 512],
                                onesf[0:1, 0:64],
                                zsb[0:1, half * 512:(half + 1) * 512],
                                start=True, stop=True)
                        sT.inc(ins)
                        mZbLast = sT.n
                        v.wait_ge(sT.h, mZbLast)
                        v.wait_ge(sAO[pA].h, sAO[pA].n)
                        sV.inc(v.tensor_tensor(out=aobuf[:, pA * QHW:(pA + 1) * QHW],
                                               in0=aot[:],
                                               in1=pbig[0:64, seq * QHW:(seq + 1) * QHW],
                                               op=ALU.mult))
                        lastEvac[seq] = sV.n
                        q.wait_ge(sV.h, lastEvac[seq])
                        sAO[pA].inc(q.dma_start(
                            out=aod[seq, h * 64:(h + 1) * 64, qh * QHW:(qh + 1) * QHW],
                            in_=aobuf[:, pA * QHW:(pA + 1) * QHW]), 16)

                units = [(qh, kt) for qh in range(NQH) for kt in range(KT)]
                kbl0 = kb_prep(*units[0])
                kbnext = kbl0
                pend = None
                pvStop = [0, 0]
                for i, (qh, kt) in enumerate(units):
                    kbl, mLn = kbnext
                    kbnext = kb_prep(*units[i + 1]) if i + 1 < len(units) else None
                    p2 = kt % 2
                    mQKs = [0, 0]
                    t.wait_ge(sQF[sl].h, loadQ)
                    t.wait_ge(sKF[sl].h, loadK)
                    for seq in range(BC):
                        sof = qof + seq * S
                        t.wait_ge(sV.h, lastBias[seq])
                        for half in range(2):
                            ins = t.matmul(
                                psB[:, seq * QHW + half * 512:
                                    seq * QHW + (half + 1) * 512],
                                kfb[0:64, sof + kt * 128: sof + (kt + 1) * 128],
                                qfb[0:64, sof + qh * QHW + half * 512:
                                    sof + qh * QHW + (half + 1) * 512],
                                start=True, stop=True)
                        sT.inc(ins)
                        mQKs[seq] = sT.n
                    mQKlast[sl] = (sT.h, sT.n)
                    if pend is not None:
                        pqh, pkt, pexp, pp2 = pend
                        t.wait_ge(sS.h, pexp)
                        if pkt == 0:
                            t.wait_ge(sV.h, lastEvac[0])
                            t.wait_ge(sV.h, lastEvac[1])
                            t.wait_ge(sV.h, mMcol)
                        for seq in range(BC):
                            vof = (seq * KT + pkt) * (NH * 65) + h * 65
                            ab = (seq * 2 + pp2) * QHW
                            for half in range(2):
                                ins = t.matmul(
                                    ppvs[seq][:, half * 512:(half + 1) * 512],
                                    v_sb2[:, vof:vof + 65],
                                    attnT[:, ab + half * 512: ab + (half + 1) * 512],
                                    start=(pkt == 0), stop=(pkt == KT - 1))
                            sT.inc(ins)
                            lastPV[seq * 2 + pp2] = sT.n
                            if pkt == KT - 1:
                                pvStop[seq] = sT.n
                        if pkt == KT - 1:
                            emit_evac(pqh, pvStop)
                    for seq in range(BC):
                        v.wait_ge(sT.h, mQKs[seq])
                        v.wait_ge(sS.h, mLn)
                        v.wait_ge(sS.h, sbtExp[p2])
                        sV.inc(v.scalar_tensor_tensor(
                            out=sbt[:, (p2 * BC + seq) * QHW:(p2 * BC + seq + 1) * QHW],
                            in0=kbl, scalar=pacols[:, 2 * h:2 * h + 1],
                            in1=psB[:, seq * QHW:(seq + 1) * QHW],
                            op0=ALU.mult, op1=ALU.add))
                        lastBias[seq] = sV.n
                    cached = (qh == 1 and kt >= 8)
                    tocache = (qh == 0 and kt < 8)
                    if cached:
                        cacheRead[kt - 8] = sV.n
                    elif tocache:
                        cacheRead[kt] = sV.n
                    else:
                        kbtBbias[kt % 2] = sV.n
                    s.wait_ge(sV.h, lastBias[1])
                    s.wait_ge(sV.h, mMcol)
                    for seq in range(BC):
                        if seq * 2 + p2 in lastPV:
                            s.wait_ge(sT.h, lastPV[seq * 2 + p2])
                    aout = AP(attnT[:].tensor, p2 * QHW,
                              [[2 * BC * QHW, 128], [2 * QHW, BC], [1, QHW]])
                    sS.inc(s.activation(out=aout,
                                        in_=sbt[:, p2 * BC * QHW:(p2 + 1) * BC * QHW],
                                        func=AF.Exp, scale=0.125,
                                        bias=mcol[:, sl:sl + 1]))
                    sbtExp[p2] = sS.n
                    pend = (qh, kt, sS.n, p2)
                pqh, pkt, pexp, pp2 = pend
                t.wait_ge(sS.h, pexp)
                for seq in range(BC):
                    vof = (seq * KT + pkt) * (NH * 65) + h * 65
                    ab = (seq * 2 + pp2) * QHW
                    for half in range(2):
                        ins = t.matmul(
                            ppvs[seq][:, half * 512:(half + 1) * 512],
                            v_sb2[:, vof:vof + 65],
                            attnT[:, ab + half * 512: ab + (half + 1) * 512],
                            start=(pkt == 0), stop=(pkt == KT - 1))
                    sT.inc(ins)
                    lastPV[seq * 2 + pp2] = sT.n
                    pvStop[seq] = sT.n
                emit_evac(pqh, pvStop)
        mEndA2 = dict(t=sT.n, v=sV.n, s=sS.n)

    with ExitStack() as a4st:
        a4sb = lambda n, sh, d: a4st.enter_context(nc.sbuf_tensor(n, sh, d))
        FCA = NH * 64 // 128
        ao_fm = a4sb("ao_fm", [128, 2 * FCA * S], BF16)
        owt_sb = a4sb("owt_sb", [128, FCA * D], BF16)
        x_sb4 = a4sb("x_sb4", [128, 2 * D], F32)
        with nc.psum_tensor("pyo", [128, 2 * 1024], F32) as pyo:
            for eng in (g, q, t, v, s):
                eng.wait_ge(sT.h, mEndA2["t"])
                eng.wait_ge(sV.h, mEndA2["v"])
                eng.wait_ge(sS.h, mEndA2["s"])
            for sm in (sQF[0], sQF[1], sKF[0], sKF[1], sAO[0], sAO[1]):
                g.wait_ge(sm.h, sm.n)
                q.wait_ge(sm.h, sm.n)
            sOW.inc(g.dma_start(
                out=owt_sb[:].rearrange("p (f w) -> p f w", f=FCA),
                in_=out_wT.rearrange("(f p) o -> p f o", p=128)), 16)
            mVadd = [0, 0]
            for seq in range(BC):
                slf = seq % 2
                sAF[slf].inc(q.dma_start(
                    out=ao_fm[:, slf * FCA * S:(slf + 1) * FCA * S]
                        .rearrange("p (f w) -> p f w", f=FCA),
                    in_=aod[seq].rearrange("(f p) w -> p f w", p=128)), 16)
                for tt in range(KT):
                    sl = tt % 2
                    q.wait_ge(sEo[sl].h, sEo[sl].n)
                    q.wait_ge(sV.h, mVadd[sl])
                    sX[sl].inc(q.dma_start(
                        out=x_sb4[:, sl * D:(sl + 1) * D],
                        in_=x_src[seq * S + tt * 128: seq * S + (tt + 1) * 128, :]), 16)
                    pyb = pyo[:, sl * 1024: sl * 1024 + 1024]
                    t.wait_ge(sAF[slf].h, sAF[slf].n)
                    t.wait_ge(sOW.h, sOW.n)
                    t.wait_ge(sV.h, mVadd[sl])
                    for oh in range(2):
                        for j in range(FCA):
                            ins = t.matmul(
                                pyb[:, oh * 512: oh * 512 + 512],
                                ao_fm[:, slf * FCA * S + j * S + tt * 128:
                                      slf * FCA * S + j * S + (tt + 1) * 128],
                                owt_sb[:, j * D + oh * 512: j * D + oh * 512 + 512],
                                start=(j == 0), stop=(j == FCA - 1))
                        sT.inc(ins)
                    mMMa = sT.n
                    v.wait_ge(sT.h, mMMa)
                    v.wait_ge(sX[sl].h, sX[sl].n)
                    for oh in range(2):
                        et = x_sb4[:, sl * D + oh * 512: sl * D + oh * 512 + 512]
                        sV.inc(v.tensor_tensor(out=et, in0=et,
                                               in1=pyb[:, oh * 512: oh * 512 + 512],
                                               op=ALU.add))
                    mVadd[sl] = sV.n
                    q.wait_ge(sV.h, mVadd[sl])
                    sEo[sl].inc(q.dma_start(
                        out=x_dst[seq * S + tt * 128: seq * S + (tt + 1) * 128, :],
                        in_=x_sb4[:, sl * D:(sl + 1) * D]), 16)
    ast.close()




def _host_inputs(cfg, L0_attn_qkv_w, L0_attn_out_w, L0_attn_bias_p, L0_attn_bias_a,
                 L0_attn_rope_freqs, L0_ffn_w_router, L0_ffn_w1, L0_ffn_b1, L0_ffn_w2,
                 L0_ffn_b2):
    import ml_dtypes
    return {
        "qkv_wT": np.ascontiguousarray(np.asarray(L0_attn_qkv_w, np.float32).T),
        "out_wT": np.ascontiguousarray(np.asarray(L0_attn_out_w, np.float32).T),
        "bias_p": np.asarray(L0_attn_bias_p, np.float32).reshape(1, cfg.NH),
        "bias_a": np.asarray(L0_attn_bias_a, np.float32).reshape(1, cfg.NH),
        "rfreqs": np.asarray(L0_attn_rope_freqs, np.float32).reshape(1, cfg.HD // 2),
        "w_router": np.ascontiguousarray(np.asarray(L0_ffn_w_router, np.float32)),
        "w1": np.ascontiguousarray(
            np.asarray(L0_ffn_w1, np.float32)
            .reshape(cfg.E, 8, 128, cfg.HC, 128).transpose(0, 3, 2, 1, 4)
            .reshape(cfg.E, cfg.HC, 128, 8 * 128)),
        "b1l": np.ascontiguousarray(
            np.asarray(L0_ffn_b1, np.float32).reshape(cfg.E, cfg.HC, 128).transpose(2, 0, 1)),
        "w2": np.ascontiguousarray(np.asarray(L0_ffn_w2, np.float32)),
        "b2": np.ascontiguousarray(np.asarray(L0_ffn_b2, np.float32)),
        "iotab": np.tile(np.arange(cfg.S, dtype=np.float32)[None, :], (128, 1)),
        "iota_row": np.arange(max(cfg.S, 128), dtype=np.float32)[None, :],
        "eyeb": np.eye(128, dtype=ml_dtypes.bfloat16),
        "eyef": np.eye(128, dtype=np.float32),
    }


def kernel(L0_attn_x, L0_attn_qkv_w, L0_attn_out_w, L0_attn_bias_p, L0_attn_bias_a,
           L0_attn_rope_freqs, L0_ffn_w_router, L0_ffn_w1, L0_ffn_b1, L0_ffn_w2,
           L0_ffn_b2):
    cfg = Cfg()
    nc = build_program(cfg)
    x = np.ascontiguousarray(np.asarray(L0_attn_x, dtype=np.float32))
    shared = _host_inputs(cfg, L0_attn_qkv_w, L0_attn_out_w, L0_attn_bias_p,
                          L0_attn_bias_a, L0_attn_rope_freqs, L0_ffn_w_router,
                          L0_ffn_w1, L0_ffn_b1, L0_ffn_w2, L0_ffn_b2)
    in_maps = []
    for core in range(8):
        m = dict(shared)
        m["x_in"] = np.ascontiguousarray(x[2 * core:2 * core + 2].reshape(cfg.T, D))
        in_maps.append(m)
    res = run_bass_kernel_spmd(nc, in_maps, core_ids=list(range(8)))
    out = np.stack([r["y_out"].reshape(2, cfg.S, D) for r in res.results])
    return out.reshape(16, cfg.S, D)
